# version 15
# baseline (speedup 1.0000x reference)
import numpy as np
from contextlib import ExitStack

import concourse.bass as bass
import concourse.mybir as mybir
from concourse.bass_utils import run_bass_kernel_spmd

F32 = mybir.dt.float32
BF16 = mybir.dt.bfloat16
I32 = mybir.dt.int32
AF = mybir.ActivationFunctionType
OP = mybir.AluOpType
AX = mybir.AxisListType.X

EPS_LN = 1e-5
EPS_WKV = 1e-8
QMAX = 127.0

DEF_CFG = dict(NSEQ=4, T=2048, W=512, C=1024, IC=2816, KC=4096, NCORE=8,
               EPS_EXACT=True, DBG=False)

MU0, RS0 = 0, 4
MU1, RS1 = 8, 12
AM1, AM2 = 16, 17
RSX, RSH = 18, 19
SXV, SHV = 20, 21
TMPA, TMPB = 22, 23
SIGT = 24
CLIPX, CLIPD = 26, 27
CEPS = 28


class Builder:
    def __init__(self, cfg):
        self.cfg = cfg
        self.NSEQ = cfg["NSEQ"]; self.T = cfg["T"]; self.W = cfg["W"]
        self.C = cfg["C"]; self.IC = cfg["IC"]; self.KC = cfg["KC"]
        self.NCH = self.C // 128
        self.ICH = self.IC // 128
        self.KCH = self.KC // 128
        self.WPS = self.T // self.W
        self.NWIN = self.NSEQ * self.WPS
        self.NTT = self.W // 128
        self.NTOK = self.NSEQ * self.T
        self.EPS_EXACT = cfg["EPS_EXACT"]
        assert self.W <= 512
        self.events = {}
        self.es = ExitStack()
        self.nc = bass.Bass("TRN2", target_bir_lowering=False)
        self.dirty = {}
        self.sems = {}
        self.sem_n = {}
        self.nbar = 0

    def sem(self, name):
        if name not in self.sems:
            self.sems[name] = self.es.enter_context(self.nc.semaphore(name))
            self.sem_n[name] = 0
        return self.sems[name]

    def mark(self, instr, semname, key=None, by=1):
        h = self.sem(semname)
        instr.then_inc(h, by)
        self.sem_n[semname] += by
        if key is not None:
            self.events[key] = (semname, self.sem_n[semname])
        return self.sem_n[semname]

    def pin_act(self, s, set_id=6):
        s.add_instruction(mybir.InstLoadActFuncSet(
            name=self.nc.get_next_instruction_name(),
            act_func_set_id=set_id, ins=[], outs=[]))

    def ms(self, e, en, semname, key):
        e.drain()
        del self.dirty.setdefault(en, [])[:]
        del self.dirty.setdefault(en + "~r", [])[:]
        return self.mark(e.nop(), semname, key)

    def setev(self, semname, key):
        self.sem(semname)
        self.events[key] = (semname, self.sem_n[semname])

    def wait_key(self, eng, key):
        semname, val = self.events[key]
        if val > 0:
            eng.wait_ge(self.sems[semname], val)

    def wait_all(self, eng, semname):
        if semname in self.sems and self.sem_n[semname] > 0:
            eng.wait_ge(self.sems[semname], self.sem_n[semname])

    def sb(self, scope, name, shape, dt):
        return scope.enter_context(self.nc.sbuf_tensor(name, shape, dt))

    @staticmethod
    def _nm(x):
        t = getattr(x, "tensor", None)
        if t is not None:
            return t.name
        return getattr(x, "name", None)

    def _ivs(self, aps):
        out = []
        for a in aps:
            if a is None or isinstance(a, (int, float)):
                continue
            t = getattr(a, "tensor", None)
            if t is not None:
                try:
                    dims = [tuple(p) for p in a.ap]
                    off = int(a.offset)
                    if len(dims) >= 2:
                        span = sum((int(n) - 1) * int(s)
                                   for s, n in dims[1:]) + 1
                    else:
                        span = (int(dims[0][1]) - 1) * int(dims[0][0]) + 1
                    out.append((t.name, off, off + span))
                    continue
                except Exception:
                    out.append((t.name, 0, 1 << 40))
                    continue
            n = self._nm(a)
            if n:
                out.append((n, 0, 1 << 40))
        return out

    @staticmethod
    def _ov(ivs, st):
        for n, lo, hi in ivs:
            for n2, lo2, hi2 in st:
                if n == n2 and lo < hi2 and lo2 < hi:
                    return True
        return False

    def dep2(self, e, en, ins, outs):
        wr = self.dirty.setdefault(en, [])
        rd = self.dirty.setdefault(en + "~r", [])
        i_n = self._ivs(ins)
        o_n = self._ivs(outs)
        if self._ov(i_n, wr) or self._ov(o_n, wr) or self._ov(o_n, rd):
            e.drain()
            del wr[:]
            del rd[:]
        rd.extend(i_n)
        wr.extend(o_n)

    def dep(self, e, en, *aps):
        d = self.dirty.setdefault(en, [])
        if self._ov(self._ivs(aps), d):
            e.drain()
            del d[:]

    def wrote(self, en, *aps):
        d = self.dirty.setdefault(en, [])
        d.extend(self._ivs(aps))

    def tt(self, e, en, out, a, b, op):
        self.dep2(e, en, (a, b), (out,))
        i = e.tensor_tensor(out, a, b, op)
        return i

    def ts(self, e, en, out, a, s1, s2, op0, op1=OP.bypass, accum_out=None):
        self.dep2(e, en, (a, s1, s2), (out, accum_out))
        i = e.tensor_scalar(out, a, s1, s2, op0, op1, accum_out=accum_out)
        return i

    def stt(self, e, en, out, a, s, b, op0, op1, accum_out=None):
        self.dep2(e, en, (a, s, b), (out, accum_out))
        i = e.scalar_tensor_tensor(out, a, s, b, op0, op1,
                                   accum_out=accum_out)
        return i

    def cp(self, e, en, out, a):
        self.dep2(e, en, (a,), (out,))
        if hasattr(e, "tensor_copy"):
            i = e.tensor_copy(out, a)
        else:
            i = e.copy(out, a)
        return i

    def scan(self, e, en, out, d0, d1, init, op0, op1):
        self.dep2(e, en, (d0, d1, init), (out,))
        i = e.tensor_tensor_scan(out, d0, d1, init, op0, op1)
        return i

    def red(self, e, en, out, a, op, absval=False):
        self.dep2(e, en, (a,), (out,))
        i = e.tensor_reduce(out, a, AX, op, apply_absolute_value=absval)
        return i

    def act(self, e, en, out, a, fn, bias=0.0, scale=1.0, accum_out=None):
        self.dep2(e, en, (a, bias, scale), (out, accum_out))
        i = e.activation(out, a, fn, bias=bias, scale=scale,
                         accum_out=accum_out)
        return i

    def mset(self, e, en, out, v):
        self.dep2(e, en, (), (out,))
        i = e.memset(out, v)
        return i

    def recip(self, e, en, out, a):
        self.dep2(e, en, (a,), (out,))
        i = e.reciprocal(out, a)
        return i

    def take_bank(self, t, ev="s"):
        b = self.bank_ptr
        self.bank_ptr = (self.bank_ptr + 1) % self.NB
        u = self.bank_use[b]
        if u > 0:
            self.wait_key(t, ("bkfree", b, u - 1))
        self.bank_use[b] = u + 1
        semname = "sc_bk" if ev == "s" else "v_bk"
        h = self.sem(semname)
        self.bkcnt = getattr(self, "bkcnt", {})
        self.bkcnt[semname] = self.bkcnt.get(semname, 0) + 1
        self.events[("bkfree", b, u)] = (semname, self.bkcnt[semname])
        return self.psb[b], b, u

    def bank_free_inc(self, instr, b, u):
        semname, val = self.events[("bkfree", b, u)]
        instr.then_inc(self.sems[semname], 1)
        self.sem_n[semname] += 1
        assert self.sem_n[semname] == val, (semname, self.sem_n[semname], val)

    def barrier(self):
        idx = self.nbar
        self.nbar += 1
        b = self.block
        name = f"bar{idx}"
        h = self.sem(name)
        pref = {"sy": ("sy_",), "v": ("v_",), "s": ("sc_",), "g": ("g_",),
                "t": ("t_",)}

        def mk(engname):
            def _(e):
                for sn in list(self.sems):
                    if sn.startswith(pref[engname]):
                        self.wait_all(e, sn)
                e.nop().then_inc(h, 1)
                e.wait_ge(h, 5)
                del self.dirty.setdefault(engname, [])[:]
                del self.dirty.setdefault(engname + "~r", [])[:]
            return _
        b.sync(mk("sy")); b.vector(mk("v")); b.scalar(mk("s"))
        b.gpsimd(mk("g")); b.tensor(mk("t"))
        self.sem_n[name] = 5

    def build(self):
        nc, es = self.nc, self.es
        C, IC, KC, NTOK = self.C, self.IC, self.KC, self.NTOK

        self.x_d = nc.dram_tensor("x", [NTOK, C], F32, kind="ExternalInput")
        self.out_d = nc.dram_tensor("out", [NTOK, C], F32,
                                    kind="ExternalOutput")
        self.wkT_d = nc.dram_tensor("wkT", [C, C], F32, kind="ExternalInput")
        self.wvT_d = nc.dram_tensor("wvT", [C, C], F32, kind="ExternalInput")
        self.wrT_d = nc.dram_tensor("wrT", [C, C], F32, kind="ExternalInput")
        self.wkcT_d = nc.dram_tensor("wkcT", [C, KC], F32,
                                     kind="ExternalInput")
        self.wvcT_d = nc.dram_tensor("wvcT", [KC, C], F32,
                                     kind="ExternalInput")
        self.wrcT_d = nc.dram_tensor("wrcT", [C, C], F32,
                                     kind="ExternalInput")
        self.wg_d = nc.dram_tensor("wg", [IC, C], F32, kind="ExternalInput")
        self.wu_d = nc.dram_tensor("wu", [IC, C], F32, kind="ExternalInput")
        self.wd_d = nc.dram_tensor("wd", [C, IC], F32, kind="ExternalInput")
        self.vecs_d = nc.dram_tensor("vecs", [128, 10 * self.NCH], F32,
                                     kind="ExternalInput")
        self.clips_d = nc.dram_tensor("clips", [1, 2], F32,
                                      kind="ExternalInput")

        kd = "ExternalOutput" if self.cfg.get("DBG") else "Internal"
        self.xn_sp = nc.dram_tensor("xn_sp", [NTOK, C], F32, kind=kd)
        self.x1_sp = nc.dram_tensor("x1_sp", [NTOK, C], F32, kind=kd)
        self.acc_sp = nc.dram_tensor("acc_sp", [NTOK, C], F32, kind=kd)
        self.q_sp = nc.dram_tensor("q_sp", [KC, NTOK], BF16, kind=kd)
        self.rc_sp = nc.dram_tensor("rc_sp", [C, NTOK], BF16, kind=kd)
        self.mid_sp = nc.dram_tensor("mid_sp", [IC, NTOK], F32, kind=kd)
        self.wgT_sp = nc.dram_tensor("wgT_sp", [C, IC], BF16, kind=kd)
        self.wuT_sp = nc.dram_tensor("wuT_sp", [C, IC], BF16, kind=kd)
        self.wdT_sp = nc.dram_tensor("wdT_sp", [IC, C], BF16, kind=kd)
        self.am_sp = nc.dram_tensor("am_sp", [1, 128], F32)
        self.sc_sp = nc.dram_tensor("sc_sp", [1, 2], F32)
        self.dbg_sp = nc.dram_tensor("dbg_sp", [128, 32], F32, kind=kd)
        self.hq_sp = nc.dram_tensor("hq_sp", [IC, NTOK], BF16, kind=kd)
        self.dwn_sp = nc.dram_tensor("dwn_sp", [C, NTOK], BF16, kind=kd)
        self.xqd_sp = nc.dram_tensor("xqd_sp", [NTOK, C], BF16, kind=kd)
        self.pf_sp = nc.dram_tensor("pf_sp", [C, self.W], BF16, kind=kd)
        self.pb_sp = nc.dram_tensor("pb_sp", [128, self.W], F32, kind=kd)
        self.pc_sp = nc.dram_tensor("pc_sp", [128, 64], F32, kind=kd)
        self.sg_sp = nc.dram_tensor("sg_sp", [IC, NTOK], BF16, kind=kd)
        self.cc1i = nc.dram_tensor("cc1i", [1, 8], F32)
        self.cc1o = nc.dram_tensor("cc1o", [1, 8], F32)
        self.cc2i = nc.dram_tensor("cc2i", [1, 8], F32)
        self.cc2o = nc.dram_tensor("cc2o", [1, 8], F32)

        NCH, W = self.NCH, self.W
        self.strips = self.sb(es, "strips", [128, 32], F32)
        self.vcs = self.sb(es, "vcs", [128, 10 * NCH], F32)
        self.eu = self.sb(es, "eu", [128, NCH], F32)
        self.carP = self.sb(es, "carP", [128, NCH], F32)
        self.carA = self.sb(es, "carA", [128, NCH], F32)
        self.sc_g = self.sb(es, "sc_g", [128, self.ICH], F32)
        self.sc_u = self.sb(es, "sc_u", [128, self.ICH], F32)
        self.sc_d = self.sb(es, "sc_d", [128, NCH], F32)
        self.cmb_g = self.sb(es, "cmb_g", [128, self.ICH], F32)
        self.cmb_u = self.sb(es, "cmb_u", [128, self.ICH], F32)
        self.cmb_d = self.sb(es, "cmb_d", [128, NCH], F32)
        self.ccb = self.sb(es, "ccb", [128, 8], F32)
        self.amrow = self.sb(es, "amrow", [1, 128], F32)

        self.NB = 8
        self.psb = [es.enter_context(self.nc.psum_tensor(f"psb{i}",
                                                         [128, 512], F32))
                    for i in range(self.NB)]
        self.bank_use = [0] * self.NB
        self.bank_ptr = 0

        self.block = es.enter_context(nc.Block())

        with ExitStack() as sc:
            self.emit_setup_and_1a(sc)
            self.emit_amax_collective(1)
            self.barrier()
        with ExitStack() as sc:
            self.emit_1bkc(sc)
            self.barrier()
        with ExitStack() as sc:
            self.emit_1bvv(sc)
            self.barrier()
        with ExitStack() as sc:
            with ExitStack() as pro:
                self.emit_wq(pro)
            self.emit_pass2(sc)
            self.emit_amax_collective(2)
            self.barrier()
        with ExitStack() as sc:
            self.emit_pass3(sc)

        es.close()
        return nc

    def emit_setup_and_1a(self, sc):
        b = self.block
        NCH, C, W, NTT, WPS = self.NCH, self.C, self.W, self.NTT, self.WPS

        self.onesb = self.sb(sc, "onesb", [128, W], BF16)
        self.wk_b = self.sb(sc, "wk_b", [128, NCH * C], BF16)
        self.wv_b = self.sb(sc, "wv_b", [128, NCH * C], BF16)
        self.wr_b = self.sb(sc, "wr_b", [128, NCH * C], BF16)
        self.x_tm = [self.sb(sc, f"a_x{i}", [128, NTT * C], F32)
                     for i in range(2)]
        self.scrbf = self.sb(sc, "a_scrbf", [128, NCH * (W + 1)], BF16)
        self.pre_f = self.sb(sc, "a_pref", [128, NCH * W], BF16)
        self.tsk = self.sb(sc, "a_tsk", [128, NCH * W], BF16)
        self.tsv = self.sb(sc, "a_tsv", [128, NCH * W], BF16)
        self.tsr = self.sb(sc, "a_tsr", [128, NCH * W], BF16)
        H = NCH // 2
        self.H = H
        self.ek_w = self.sb(sc, "a_ekw", [128, NCH * W], BF16)
        self.v_w = self.sb(sc, "a_vw", [128, NCH * W], BF16)
        self.er_h = self.sb(sc, "a_erh", [128, H * W], BF16)
        self.wP_h = self.sb(sc, "a_wPh", [128, H * (W + 1)], BF16)
        self.wA = self.sb(sc, "a_wA", [128, H * (W + 1)], F32)
        self.wb = [self.sb(sc, f"a_wb{i}", [128, H * W], BF16)
                   for i in range(2)]
        self.wf = [self.sb(sc, f"a_wf{i}", [128, H * W], F32)
                   for i in range(2)]
        self.amxa = self.sb(sc, "a_amxa", [128, 8], F32)
        self.LNW = self.sb(sc, "a_LNW", [128, H * W], F32)
        self.LND = self.sb(sc, "a_LND", [128, H * W], F32)
        self.rw_h = self.sb(sc, "a_rwh", [128, H * W], BF16)
        self.hcar = self.sb(sc, "a_hcar", [128, NCH], F32)

        @b.gpsimd
        def _(g):
            for wd, wbuf in ((self.wkT_d, self.wk_b), (self.wvT_d, self.wv_b),
                             (self.wrT_d, self.wr_b)):
                for cc in range(NCH):
                    i = g.dma_start(out=wbuf[:, cc * C:(cc + 1) * C],
                                    in_=wd[cc * 128:(cc + 1) * 128, :])
                    self.mark(i, "g_ld", by=16)
            self.setev("g_ld", ("wproj",))

        @b.sync
        def _(sy):
            i = sy.dma_start(out=self.vcs[:, :], in_=self.vecs_d[:, :])
            self.mark(i, "sy_ld", by=16)
            i = sy.dma_start(out=self.strips[0:1, CLIPX:CLIPD + 1],
                             in_=self.clips_d[0:1, 0:2])
            self.mark(i, "sy_ld", by=16)
            self.setev("sy_ld", ("vecs",))
            self.setev("sy_ld", ("clips",))

        @b.vector
        def _(v):
            self.mset(v, "v", self.onesb[:, :], 1.0)
            self.mset(v, "v", self.carP[:, :], 0.0)
            self.mset(v, "v", self.carA[:, :], 0.0)
            self.mset(v, "v", self.strips[:, CEPS:CEPS + 1], EPS_LN)
            self.mset(v, "v", self.ccb[:, :], 0.0)
            self.mark(v.nop(), "v_ms", ("setupv",))

        @b.scalar
        def _(s):
            self.pin_act(s)
            self.wait_key(s, ("vecs",))
            self.act(s, "s", self.eu[:, :],
                     self.vcs[:, 9 * NCH:10 * NCH], AF.Exp)
            self.mark(s.nop(), "sc_ms", ("setups",))

        for w in range(self.NWIN):
            self.win_1a(w)

    def ln_stats_v(self, v, src, dummy, mu0, rs0):
        st, C, NTT = self.strips, self.C, self.NTT
        LT = (TMPA, TMPB, 25, 29)
        for t in range(NTT):
            self.ts(v, "v", dummy[:, t * C:(t + 1) * C],
                    src[:, t * C:(t + 1) * C], 1.0, None, OP.mult, OP.add,
                    accum_out=st[:, mu0 + t:mu0 + t + 1])
        for t in range(NTT):
            xt = src[:, t * C:(t + 1) * C]
            self.stt(v, "v", dummy[:, t * C:(t + 1) * C], xt, 1.0, xt,
                     OP.mult, OP.mult,
                     accum_out=st[:, rs0 + t:rs0 + t + 1])
        for t in range(NTT):
            self.ts(v, "v", st[:, mu0 + t:mu0 + t + 1],
                    st[:, mu0 + t:mu0 + t + 1], 1.0 / C, None, OP.mult)
        for t in range(NTT):
            self.ts(v, "v", st[:, rs0 + t:rs0 + t + 1],
                    st[:, rs0 + t:rs0 + t + 1], 1.0 / C, None, OP.mult)
        for t in range(NTT):
            self.stt(v, "v", st[:, LT[t]:LT[t] + 1],
                     st[:, mu0 + t:mu0 + t + 1], -1.0,
                     st[:, mu0 + t:mu0 + t + 1], OP.mult, OP.mult)
        for t in range(NTT):
            self.tt(v, "v", st[:, rs0 + t:rs0 + t + 1],
                    st[:, rs0 + t:rs0 + t + 1], st[:, LT[t]:LT[t] + 1],
                    OP.add)

    def ln_rs_s(self, s, rs0):
        sl = self.strips[:, rs0:rs0 + self.NTT]
        self.act(s, "s", sl, sl, AF.Ln,
                 bias=self.strips[:, CEPS:CEPS + 1])
        self.act(s, "s", sl, sl, AF.Exp, scale=-0.5)

    def win_1a(self, w):
        b = self.block
        NCH, C, W, NTT, WPS = self.NCH, self.C, self.W, self.NTT, self.WPS
        H = self.H
        seq, wj = divmod(w, WPS)
        buf = w % 2
        st = self.strips
        tr0 = w * W // 128
        WL = W + 1
        pre_bf = self.scrbf
        h_f = self.scrbf
        rw_tm = self.pre_f

        @b.sync
        def _(sy):
            if w >= 2:
                self.wait_key(sy, ("x1sp", w - 2))
            i = sy.dma_start(
                out=self.x_tm[buf].ap().rearrange("p (t c) -> p t c", c=C),
                in_=self.x_d.ap().rearrange("(t p) c -> p t c", p=128)[
                    :, tr0:tr0 + NTT, :])
            self.mark(i, f"sy_x{buf}", ("xld", w), by=16)

        @b.vector
        def _(v):
            if w == 0:
                self.wait_key(v, ("setupv",))
                self.wait_key(v, ("wproj",))
                self.wait_key(v, ("vecs",))
            if w >= 1:
                self.wait_key(v, ("xnsp", w - 1))
            self.wait_key(v, ("xld", w))
            self.ln_stats_v(v, self.x_tm[buf], self.pre_f, MU0, RS0)
            self.ms(v, "v", "v_ms", ("lnov", w))

        @b.scalar
        def _(s):
            if w == 0:
                self.wait_key(s, ("setups",))
            self.wait_key(s, ("lnov", w))
            self.ln_rs_s(s, RS0)
            self.ms(s, "s", "sc_ms", ("lnos", w))

        @b.vector
        def _(v):
            self.wait_key(v, ("lnos", w))
            for t in range(NTT):
                self.ts(v, "v", self.x_tm[buf][:, t * C:(t + 1) * C],
                        self.x_tm[buf][:, t * C:(t + 1) * C],
                        st[:, MU0 + t:MU0 + t + 1], st[:, RS0 + t:RS0 + t + 1],
                        OP.subtract, OP.mult)
            for t in range(NTT):
                self.red(v, "v", self.amxa[:, t:t + 1],
                         self.x_tm[buf][:, t * C:(t + 1) * C], OP.max,
                         absval=True)
            self.red(v, "v", st[:, TMPA:TMPA + 1], self.amxa[:, 0:NTT],
                     OP.max)
            if w == 0:
                self.cp(v, "v", st[:, AM1:AM1 + 1], st[:, TMPA:TMPA + 1])
            else:
                self.tt(v, "v", st[:, AM1:AM1 + 1], st[:, AM1:AM1 + 1],
                        st[:, TMPA:TMPA + 1], OP.max)
            self.ms(v, "v", "v_ms", ("xn", w))
            self.ln_stats_v(v, self.x_tm[buf], self.pre_f, MU1, RS1)
            self.ms(v, "v", "v_ms", ("ln1v", w))

        @b.sync
        def _(sy):
            self.wait_key(sy, ("xn", w))
            i = sy.dma_start(
                out=self.xn_sp.ap().rearrange("(t p) c -> p t c", p=128)[
                    :, tr0:tr0 + NTT, :],
                in_=self.x_tm[buf].ap().rearrange("p (t c) -> p t c", c=C))
            self.mark(i, "sy_xnsp", ("xnsp", w), by=16)

        @b.scalar
        def _(s):
            self.wait_key(s, ("ln1v", w))
            self.ln_rs_s(s, RS1)
            self.ms(s, "s", "sc_ms", ("ln1s", w))

        @b.vector
        def _(v):
            self.wait_key(v, ("ln1s", w))
            for t in range(NTT):
                self.ts(v, "v", pre_bf[:, t * C:(t + 1) * C],
                        self.x_tm[buf][:, t * C:(t + 1) * C],
                        st[:, MU1 + t:MU1 + t + 1], st[:, RS1 + t:RS1 + t + 1],
                        OP.subtract, OP.mult)
            self.ms(v, "v", "v_ms", ("pre1", w))

        @b.scalar
        def _(s):
            self.wait_key(s, ("pre1", w))
            for t in range(NTT):
                i = s.dma_start(
                    out=self.pre_f.ap().rearrange(
                        "p (c q) -> p c q", q=W)[:, :, t * 128:(t + 1) * 128],
                    in_=pre_bf.ap()[:, 0:NTT * C].rearrange(
                        "p (t c q) -> p (t c) q", q=128, c=NCH)[
                        :, t * NCH:(t + 1) * NCH, :],
                    transpose=True)
                self.mark(i, "sc_preT", by=16)
            self.setev("sc_preT", ("preT", w))

        @b.vector
        def _(v):
            vcs = self.vcs
            self.wait_key(v, ("preT", w))
            if w >= 1:
                self.wait_key(v, ("tsdone", w - 1))
            for c in range(NCH):
                self.ts(v, "v", h_f[:, c * WL + 1:(c + 1) * WL],
                        self.pre_f[:, c * W:(c + 1) * W],
                        vcs[:, 0 * NCH + c:0 * NCH + c + 1],
                        vcs[:, 1 * NCH + c:1 * NCH + c + 1],
                        OP.mult, OP.add)
            if wj == 0:
                for c in range(NCH):
                    self.mset(v, "v", h_f[:, c * WL:c * WL + 1], 0.0)
            else:
                for c in range(NCH):
                    self.cp(v, "v", h_f[:, c * WL:c * WL + 1],
                            self.hcar[:, c:c + 1])
            hv = h_f.ap()[:, 0:NCH * WL].rearrange("p (c q) -> p c q", q=WL)
            pf = self.pre_f.ap().rearrange("p (c q) -> p c q", q=W)
            self.tt(v, "v", pf, hv[:, :, 1:WL], hv[:, :, 0:W], OP.subtract)
            for dst, mrow in ((self.tsk, 4), (self.tsv, 5), (self.tsr, 6)):
                for c in range(NCH):
                    self.stt(v, "v", dst[:, c * W:(c + 1) * W],
                             self.pre_f[:, c * W:(c + 1) * W],
                             vcs[:, mrow * NCH + c:mrow * NCH + c + 1],
                             h_f[:, c * WL:c * WL + W],
                             OP.mult, OP.add)
            if wj != WPS - 1:
                for c in range(NCH):
                    self.cp(v, "v", self.hcar[:, c:c + 1],
                            h_f[:, c * WL + W:c * WL + W + 1])
            self.ms(v, "v", "v_ms", ("ts", w))

        @b.tensor
        def _(t):
            self.wait_key(t, ("ts", w))
            for c in range(NCH):
                for q, (wbuf, tsb) in enumerate(
                        ((self.wk_b, self.tsk), (self.wv_b, self.tsv),
                         (self.wr_b, self.tsr))):
                    pt, bk, u = self.take_bank(t, ev="s")
                    for cc in range(NCH):
                        i = t.matmul(
                            pt[:, 0:W],
                            wbuf[:, cc * C + c * 128:cc * C + (c + 1) * 128],
                            tsb[:, cc * W:(cc + 1) * W],
                            start=(cc == 0), stop=(cc == NCH - 1))
                    self.mark(i, "t_mm", ("mm", bk, u))
                    self.events[("pbank", w, c, q)] = (bk, u)
            self.ms(t, "t", "t_ms", ("tsdone", w))

        def evac_half(s, h):
            for c in range(h * H, (h + 1) * H):
                hc = c - h * H
                for q in range(3):
                    bk, u = self.events[("pbank", w, c, q)]
                    self.wait_key(s, ("mm", bk, u))
                    pt = self.psb[bk]
                    if q == 0:
                        i = self.act(s, "s", self.ek_w[:, c * W:(c + 1) * W],
                                     pt[:, 0:W], AF.Exp)
                    elif q == 1:
                        i = self.cp(s, "s", self.v_w[:, c * W:(c + 1) * W],
                                    pt[:, 0:W])
                    else:
                        i = self.act(s, "s",
                                     self.er_h[:, hc * W:(hc + 1) * W],
                                     pt[:, 0:W], AF.Exp, scale=-1.0)
                    self.bank_free_inc(i, bk, u)
            self.ms(s, "s", "sc_ms", ("evh", w, h))

        @b.scalar
        def _(s):
            evac_half(s, 0)

        @b.vector
        def _(v):
            self.wait_key(v, ("evh", w, 0))
            if wj == 0:
                self.mset(v, "v", self.carP[:, :], 0.0)
                self.mset(v, "v", self.carA[:, :], 0.0)
            self.wkv_part1_half(v, w, 0)
            self.ms(v, "v", "v_ms", ("p1", w, 0))

        @b.scalar
        def _(s):
            self.wait_key(s, ("p1", w, 0))
            evac_half(s, 1)
            self.lnexp_half(s, w, 0)

        @b.vector
        def _(v):
            self.wait_key(v, ("lex", w, 0))
            if w >= 1:
                self.wait_key(v, ("rwT", w - 1, 1))
            self.wkv_part2_half(v, w, 0)
            self.ms(v, "v", "v_ms", ("p2", w, 0))

        @b.sync
        def _(sy):
            self.wait_key(sy, ("p2", w, 0))
            for hc in range(H):
                i = sy.dma_start(
                    out=rw_tm.ap().rearrange("p (t cq) -> p t cq", cq=C)[
                        :, :, hc * 128:(hc + 1) * 128],
                    in_=self.rw_h[:, hc * W:(hc + 1) * W].rearrange(
                        "p (t q) -> p t q", q=128),
                    transpose=True)
                self.mark(i, "sy_rwT0", by=16)
            self.setev("sy_rwT0", ("rwT", w, 0))

        @b.vector
        def _(v):
            self.wait_key(v, ("evh", w, 1))
            self.wait_key(v, ("p2", w, 0))
            self.wkv_part1_half(v, w, 1)
            self.ms(v, "v", "v_ms", ("p1", w, 1))

        @b.scalar
        def _(s):
            self.wait_key(s, ("p1", w, 1))
            self.lnexp_half(s, w, 1)

        @b.vector
        def _(v):
            self.wait_key(v, ("lex", w, 1))
            self.wait_key(v, ("rwT", w, 0))
            self.wkv_part2_half(v, w, 1)
            self.ms(v, "v", "v_ms", ("p2", w, 1))

        @b.sync
        def _(sy):
            self.wait_key(sy, ("p2", w, 1))
            for hc in range(H):
                c = H + hc
                i = sy.dma_start(
                    out=rw_tm.ap().rearrange("p (t cq) -> p t cq", cq=C)[
                        :, :, c * 128:(c + 1) * 128],
                    in_=self.rw_h[:, hc * W:(hc + 1) * W].rearrange(
                        "p (t q) -> p t q", q=128),
                    transpose=True)
                self.mark(i, "sy_rwT1", by=16)
            self.setev("sy_rwT1", ("rwT", w, 1))

        @b.vector
        def _(v):
            self.wait_key(v, ("rwT", w, 1))
            self.wait_key(v, ("xnsp", w))
            for t in range(NTT):
                self.tt(v, "v", self.x_tm[buf][:, t * C:(t + 1) * C],
                        self.x_tm[buf][:, t * C:(t + 1) * C],
                        rw_tm[:, t * C:(t + 1) * C], OP.add)
            self.ms(v, "v", "v_ms", ("x1", w))

        @b.sync
        def _(sy):
            self.wait_key(sy, ("x1", w))
            i = sy.dma_start(
                out=self.x1_sp.ap().rearrange("(t p) c -> p t c", p=128)[
                    :, tr0:tr0 + NTT, :],
                in_=self.x_tm[buf].ap().rearrange("p (t c) -> p t c", c=C))
            self.mark(i, f"sy_x1sp{buf}", ("x1sp", w), by=16)

        if w == self.NWIN - 1:
            @b.vector
            def _(v):
                self.mark(v.nop(), "v_ms", ("amax", 1))

    def lnexp_half(self, s, w, h):
        H, W = self.H, self.W
        sl = slice(0, H * W)
        self.act(s, "s", self.LNW[:, sl], self.LNW[:, sl], AF.Ln)
        self.act(s, "s", self.LNW[:, sl], self.LNW[:, sl], AF.Exp, scale=-1.0)
        self.act(s, "s", self.LND[:, sl], self.LND[:, sl], AF.Ln)
        self.act(s, "s", self.LND[:, sl], self.LND[:, sl], AF.Exp, scale=-1.0)
        self.ms(s, "s", "sc_ms", ("lex", w, h))

    def wkv_part1_half(self, v, w, h):
        H, W, WL = self.H, self.W, self.W + 1
        cs = list(range(h * H, (h + 1) * H))

        def E(hc):
            c = cs[hc]
            ek = self.ek_w[:, c * W:(c + 1) * W]
            er = self.er_h[:, hc * W:(hc + 1) * W]
            wPt = self.wP_h[:, hc * WL:(hc + 1) * WL]
            wP = self.wP_h[:, hc * WL:(hc + 1) * WL]
            w0 = self.wb[0][:, hc * W:(hc + 1) * W]
            w1 = self.wb[1][:, hc * W:(hc + 1) * W]
            w2 = None
            Mw = self.LNW[:, hc * W:(hc + 1) * W]
            ddw = self.LND[:, hc * W:(hc + 1) * W]
            return c, ek, er, wPt, wP, w0, w1, w2, Mw, ddw

        def row(fn):
            for hc in range(H):
                fn(*E(hc))
        row(lambda c, ek, er, wPt, wP, w0, w1, w2, Mw, ddw:
            self.cp(v, "v", wP[:, 0:1], self.carP[:, c:c + 1]))
        row(lambda c, ek, er, wPt, wP, w0, w1, w2, Mw, ddw:
            self.scan(v, "v", wP[:, 1:WL], self.onesb[:, :], ek,
                      self.carP[:, c:c + 1], OP.mult, OP.add))
        row(lambda c, ek, er, wPt, wP, w0, w1, w2, Mw, ddw:
            self.cp(v, "v", self.carP[:, c:c + 1], wP[:, W:WL]))
        row(lambda c, ek, er, wPt, wP, w0, w1, w2, Mw, ddw:
            self.tt(v, "v", Mw, wP[:, 0:W], ek, OP.max))
        row(lambda c, ek, er, wPt, wP, w0, w1, w2, Mw, ddw:
            self.ts(v, "v", w0, ek, self.eu[:, c:c + 1], None, OP.mult))
        row(lambda c, ek, er, wPt, wP, w0, w1, w2, Mw, ddw:
            self.tt(v, "v", w1, wP[:, 0:W], w0, OP.add))
        row(lambda c, ek, er, wPt, wP, w0, w1, w2, Mw, ddw:
            self.stt(v, "v", ddw, er, 1.0, w1, OP.add, OP.mult))

    def wkv_part2_half(self, v, w, h):
        H, W, WL = self.H, self.W, self.W + 1
        cs = list(range(h * H, (h + 1) * H))

        def E(hc):
            c = cs[hc]
            ek = self.ek_w[:, c * W:(c + 1) * W]
            vv = self.v_w[:, c * W:(c + 1) * W]
            wP = self.wP_h[:, hc * WL:(hc + 1) * WL]
            wA = self.wA[:, hc * WL:(hc + 1) * WL]
            w0 = self.wb[0][:, hc * W:(hc + 1) * W]
            wf1 = self.wf[0][:, hc * W:(hc + 1) * W]
            wf2 = self.wf[1][:, hc * W:(hc + 1) * W]
            rM = self.LNW[:, hc * W:(hc + 1) * W]
            rdd = self.LND[:, hc * W:(hc + 1) * W]
            rw = self.rw_h[:, hc * W:(hc + 1) * W]
            return c, ek, vv, wP, wA, w0, wf1, wf2, rM, rdd, rw

        def row(fn):
            for hc in range(H):
                fn(*E(hc))
        row(lambda c, ek, vv, wP, wA, w0, wf1, wf2, rM, rdd, rw:
            self.tt(v, "v", wf1, wP[:, 0:W], rM, OP.mult))
        row(lambda c, ek, vv, wP, wA, w0, wf1, wf2, rM, rdd, rw:
            self.tt(v, "v", wf2, ek, rM, OP.mult))
        row(lambda c, ek, vv, wP, wA, w0, wf1, wf2, rM, rdd, rw:
            self.tt(v, "v", wf2, wf2, vv, OP.mult))
        row(lambda c, ek, vv, wP, wA, w0, wf1, wf2, rM, rdd, rw:
            self.cp(v, "v", wA[:, 0:1], self.carA[:, c:c + 1]))
        row(lambda c, ek, vv, wP, wA, w0, wf1, wf2, rM, rdd, rw:
            self.scan(v, "v", wA[:, 1:WL], wf1, wf2,
                      self.carA[:, c:c + 1], OP.mult, OP.add))
        row(lambda c, ek, vv, wP, wA, w0, wf1, wf2, rM, rdd, rw:
            self.cp(v, "v", self.carA[:, c:c + 1], wA[:, W:WL]))
        row(lambda c, ek, vv, wP, wA, w0, wf1, wf2, rM, rdd, rw:
            self.tt(v, "v", wf1, wP[:, 0:W], wA[:, 0:W], OP.mult))
        row(lambda c, ek, vv, wP, wA, w0, wf1, wf2, rM, rdd, rw:
            self.tt(v, "v", wf2, w0, vv, OP.mult))
        row(lambda c, ek, vv, wP, wA, w0, wf1, wf2, rM, rdd, rw:
            self.tt(v, "v", wf1, wf1, wf2, OP.add))
        row(lambda c, ek, vv, wP, wA, w0, wf1, wf2, rM, rdd, rw:
            self.tt(v, "v", rw, wf1, rdd, OP.mult))

    def emit_amax_collective(self, which):
        b = self.block
        st = self.strips
        amcol = AM1 if which == 1 else AM2
        rscol = RSX if which == 1 else RSH
        svcol = SXV if which == 1 else SHV
        clipcol = CLIPX if which == 1 else CLIPD
        cci = self.cc1i if which == 1 else self.cc2i
        cco = self.cc1o if which == 1 else self.cc2o

        @b.sync
        def _(sy):
            self.wait_key(sy, ("amax", which))
            i = sy.dma_start(
                out=self.am_sp[0:1, :].rearrange("o (p q) -> p (o q)", q=1),
                in_=st[:, amcol:amcol + 1])
            self.mark(i, "sy_am", by=16)
            self.wait_all(sy, "sy_am")
            i = sy.dma_start(out=self.amrow[:, :], in_=self.am_sp[0:1, :])
            self.mark(i, "sy_am", ("amrow", which), by=16)

        @b.vector
        def _(v):
            self.wait_key(v, ("amrow", which))
            self.red(v, "v", self.ccb[0:1, 0:1], self.amrow[:, :], OP.max)
            self.ms(v, "v", "v_ms", ("amloc", which))

        @b.gpsimd
        def _(g):
            self.wait_key(g, ("amloc", which))
            i = g.dma_start(out=cci[0:1, 0:8], in_=self.ccb[0:1, 0:8])
            self.mark(i, "g_cd", by=16)
            self.wait_all(g, "g_cd")
            i = g.collective_compute(
                "AllReduce", OP.max,
                replica_groups=[list(range(self.cfg["NCORE"]))],
                ins=[cci.ap().opt()], outs=[cco.ap().opt()])
            self.mark(i, "g_cc", by=1)
            self.wait_all(g, "g_cc")
            i = g.dma_start(out=self.ccb[0:1, 1:2], in_=cco[0:1, 0:1])
            self.mark(i, "g_cd", ("ccdone", which), by=16)

        @b.scalar
        def _(s):
            if which == 2:
                self.pin_act(s, set_id=6)
            self.wait_key(s, ("ccdone", which))
            self.wait_key(s, ("clips",))
            cl = st[0:1, clipcol:clipcol + 1]
            t0 = st[0:1, SIGT:SIGT + 1]
            self.act(s, "s", t0, cl, AF.Exp, scale=-1.0)
            self.act(s, "s", t0, t0, AF.Ln, bias=1.0)
            self.act(s, "s", t0, t0, AF.Exp, scale=-1.0)
            self.ms(s, "s", "sc_ms", ("sig", which))

        @b.vector
        def _(v):
            self.wait_key(v, ("sig", which))
            sv = st[0:1, svcol:svcol + 1]
            self.tt(v, "v", sv, st[0:1, SIGT:SIGT + 1], self.ccb[0:1, 1:2],
                    OP.mult)
            self.ts(v, "v", sv, sv, 1.0 / QMAX, None, OP.mult)
            self.recip(v, "v", st[0:1, rscol:rscol + 1], sv)
            self.ms(v, "v", "v_ms", ("scal0", which))

        @b.sync
        def _(sy):
            self.wait_key(sy, ("scal0", which))
            i = sy.dma_start(out=self.sc_sp[0:1, 0:1],
                             in_=st[0:1, svcol:svcol + 1])
            self.mark(i, "sy_am", by=16)
            i = sy.dma_start(out=self.sc_sp[0:1, 1:2],
                             in_=st[0:1, rscol:rscol + 1])
            self.mark(i, "sy_am", by=16)
            self.wait_all(sy, "sy_am")
            i = sy.dma_start(out=st[:, svcol:svcol + 1],
                             in_=self.sc_sp[0:1, 0:1].broadcast_to((128, 1)))
            self.mark(i, "sy_am", by=16)
            i = sy.dma_start(out=st[:, rscol:rscol + 1],
                             in_=self.sc_sp[0:1, 1:2].broadcast_to((128, 1)))
            self.mark(i, "sy_am", by=16)
            self.wait_all(sy, "sy_am")
            i = sy.nop()
            self.mark(i, "sy_sc", ("scales", which))

    def emit_1bkc(self, sc):
        b = self.block
        NCH, KCH, C, KC, W, NTT, WPS = (self.NCH, self.KCH, self.C, self.KC,
                                        self.W, self.NTT, self.WPS)

        self.wk_b2 = self.sb(sc, "wk_b2", [128, NCH * KC], BF16)
        self.wr_b2 = self.sb(sc, "wr_b2", [128, NCH * C], BF16)
        self.x_tm = [self.sb(sc, f"b_x{i}", [128, NTT * C], F32)
                     for i in range(2)]
        self.pre_bf = self.sb(sc, "b_pre", [128, NTT * C], BF16)
        self.pre_f = self.sb(sc, "b_pref", [128, NCH * W], BF16)
        self.h_f = self.sb(sc, "b_h", [128, NCH * (W + 1)], BF16)
        self.tsk2 = [self.sb(sc, f"b_tsk{i}", [128, NCH * W], BF16)
                     for i in range(2)]
        self.tsv2 = [self.sb(sc, f"b_tsv{i}", [128, NCH * W], BF16)
                     for i in range(2)]
        self.rcb = self.sb(sc, "b_rc", [128, NCH * W], BF16)
        self.rcs = self.sb(sc, "b_rcs", [128, NCH * W], BF16)
        self.kcb = [self.sb(sc, f"b_kc{i}", [128, W], BF16)
                    for i in range(16)]

        @b.gpsimd
        def _(g):
            for cc in range(NCH):
                i = g.dma_start(out=self.wk_b2[:, cc * KC:(cc + 1) * KC],
                                in_=self.wkcT_d[cc * 128:(cc + 1) * 128, :])
                self.mark(i, "g_ld", by=16)
                i = g.dma_start(out=self.wr_b2[:, cc * C:(cc + 1) * C],
                                in_=self.wrcT_d[cc * 128:(cc + 1) * 128, :])
                self.mark(i, "g_ld", by=16)
            self.setev("g_ld", ("wkc",))

        for w in range(self.NWIN):
            self.win_1bkc(w)

    def win_1bkc(self, w):
        b = self.block
        NCH, KCH, C, W, NTT, WPS = (self.NCH, self.KCH, self.C, self.W,
                                    self.NTT, self.WPS)
        seq, wj = divmod(w, WPS)
        buf = w % 2
        st = self.strips
        tr0 = w * W // 128

        @b.sync
        def _(sy):
            if w >= 2:
                self.wait_key(sy, ("pre2", w - 2))
            i = sy.dma_start(
                out=self.x_tm[buf].ap().rearrange("p (t c) -> p t c", c=C),
                in_=self.x1_sp.ap().rearrange("(t p) c -> p t c", p=128)[
                    :, tr0:tr0 + NTT, :])
            self.mark(i, f"sy_x{buf}", ("x1ld", w), by=16)

        @b.vector
        def _(v):
            self.wait_key(v, ("x1ld", w))
            if w == 0:
                self.wait_key(v, ("wkc",))
            if w >= 1:
                self.wait_key(v, ("preT2", w - 1))
            self.ln_stats_v(v, self.x_tm[buf], self.pre_bf, MU1, RS1)
            self.ms(v, "v", "v_ms", ("ln2v", w))

        @b.scalar
        def _(s):
            self.wait_key(s, ("ln2v", w))
            self.ln_rs_s(s, RS1)
            self.ms(s, "s", "sc_ms", ("ln2s", w))

        @b.vector
        def _(v):
            self.wait_key(v, ("ln2s", w))
            for t in range(NTT):
                self.ts(v, "v", self.pre_bf[:, t * C:(t + 1) * C],
                        self.x_tm[buf][:, t * C:(t + 1) * C],
                        st[:, MU1 + t:MU1 + t + 1], st[:, RS1 + t:RS1 + t + 1],
                        OP.subtract, OP.mult)
            self.ms(v, "v", "v_ms", ("pre2", w))

        @b.scalar
        def _(s):
            self.wait_key(s, ("pre2", w))
            for t in range(NTT):
                i = s.dma_start(
                    out=self.pre_f.ap().rearrange(
                        "p (c q) -> p c q", q=W)[:, :, t * 128:(t + 1) * 128],
                    in_=self.pre_bf.ap().rearrange(
                        "p (t c q) -> p (t c) q", q=128, c=NCH)[
                        :, t * NCH:(t + 1) * NCH, :],
                    transpose=True)
                self.mark(i, "sc_preT", by=16)
            self.setev("sc_preT", ("preT2", w))

        @b.vector
        def _(v):
            vcs = self.vcs
            self.wait_key(v, ("preT2", w))
            if w >= 2:
                self.wait_key(v, ("tsdone2", w - 2))
            tsk, tsv = self.tsk2[buf], self.tsv2[buf]
            for c in range(NCH):
                self.ts(v, "v", self.h_f[:, c * (W + 1) + 1:(c + 1) * (W + 1)],
                        self.pre_f[:, c * W:(c + 1) * W],
                        vcs[:, 2 * NCH + c:2 * NCH + c + 1],
                        vcs[:, 3 * NCH + c:3 * NCH + c + 1],
                        OP.mult, OP.add)
            if wj == 0:
                for c in range(NCH):
                    self.mset(v, "v",
                              self.h_f[:, c * (W + 1):c * (W + 1) + 1], 0.0)
            hv = self.h_f.ap().rearrange("p (c q) -> p c q", q=W + 1)
            pf = self.pre_f.ap().rearrange("p (c q) -> p c q", q=W)
            self.tt(v, "v", pf, hv[:, :, 1:W + 1], hv[:, :, 0:W], OP.subtract)
            for dst, mrow in ((tsk, 7), (tsv, 8)):
                for c in range(NCH):
                    self.stt(v, "v", dst[:, c * W:(c + 1) * W],
                             self.pre_f[:, c * W:(c + 1) * W],
                             vcs[:, mrow * NCH + c:mrow * NCH + c + 1],
                             self.h_f[:, c * (W + 1):c * (W + 1) + W],
                             OP.mult, OP.add)
            if wj != WPS - 1:
                for c in range(NCH):
                    self.cp(v, "v", self.h_f[:, c * (W + 1):c * (W + 1) + 1],
                            self.h_f[:, c * (W + 1) + W:c * (W + 1) + W + 1])
            self.ms(v, "v", "v_ms", ("ts2", w))

        @b.tensor
        def _(t):
            self.wait_key(t, ("ts2", w))
            for c in range(NCH):
                pt, bk, u = self.take_bank(t, ev="s")
                for cc in range(NCH):
                    i = t.matmul(pt[:, 0:W],
                                 self.wr_b2[:, cc * C + c * 128:
                                            cc * C + (c + 1) * 128],
                                 self.tsv2[buf][:, cc * W:(cc + 1) * W],
                                 start=(cc == 0), stop=(cc == NCH - 1))
                self.mark(i, "t_mm", ("mm", bk, u))
                self.events[("rcbank", w, c)] = (bk, u)
            for c in range(self.KCH):
                pt, bk, u = self.take_bank(t, ev="s")
                for cc in range(NCH):
                    i = t.matmul(pt[:, 0:W],
                                 self.wk_b2[:, cc * self.KC + c * 128:
                                            cc * self.KC + (c + 1) * 128],
                                 self.tsk2[buf][:, cc * W:(cc + 1) * W],
                                 start=(cc == 0), stop=(cc == NCH - 1))
                self.mark(i, "t_mm", ("mm", bk, u))
                self.events[("kcbank", w, c)] = (bk, u)
            self.ms(t, "t", "t_ms", ("tsdone2", w))

        @b.scalar
        def _(s):
            for c in range(NCH):
                bk, u = self.events[("rcbank", w, c)]
                self.wait_key(s, ("mm", bk, u))
                pt = self.psb[bk]
                i = self.act(s, "s", self.rcb[:, c * W:(c + 1) * W],
                             pt[:, 0:W], AF.Exp, scale=-1.0)
                self.bank_free_inc(i, bk, u)
            self.ms(s, "s", "sc_ms", ("rce", w))
            if w >= 1:
                self.wait_key(s, ("qsp1", w - 1))
            for c in range(16):
                bk, u = self.events[("kcbank", w, c)]
                self.wait_key(s, ("mm", bk, u))
                pt = self.psb[bk]
                i = self.act(s, "s", self.kcb[c][:, :], pt[:, 0:W], AF.Relu)
                self.bank_free_inc(i, bk, u)
            self.ms(s, "s", "sc_ms", ("kcr", w, 0))

        @b.vector
        def _(v):
            self.wait_key(v, ("rce", w))
            if w >= 1:
                self.wait_key(v, ("rcsp", w - 1))
            self.ts(v, "v", self.rcs[:, :], self.rcb[:, :], 1.0, None, OP.add)
            self.ms(v, "v", "v_ms", ("rc1", w))
            self.wait_key(v, ("kcr", w, 0))
            for c in range(16):
                self.tt(v, "v", self.kcb[c][:, :], self.kcb[c][:, :],
                        self.kcb[c][:, :], OP.mult)
            self.ms(v, "v", "v_ms", ("kcq", w, 0))

        @b.sync
        def _(sy):
            self.wait_key(sy, ("kcq", w, 0))
            for c in range(16):
                i = sy.dma_start(
                    out=self.q_sp[c * 128:(c + 1) * 128, w * W:(w + 1) * W],
                    in_=self.kcb[c][:, :])
                self.mark(i, "sy_qsp0", by=16)
            self.setev("sy_qsp0", ("qsp0", w))

        @b.scalar
        def _(s):
            self.wait_key(s, ("rc1", w))
            self.act(s, "s", self.rcb[:, :], self.rcs[:, :], AF.Ln)
            self.act(s, "s", self.rcs[:, :], self.rcb[:, :], AF.Exp,
                     scale=-1.0)
            self.ms(s, "s", "sc_ms", ("rc", w))
            self.wait_key(s, ("qsp0", w))
            for c in range(16, self.KCH):
                bk, u = self.events[("kcbank", w, c)]
                self.wait_key(s, ("mm", bk, u))
                pt = self.psb[bk]
                i = self.act(s, "s", self.kcb[c - 16][:, :], pt[:, 0:W],
                             AF.Relu)
                self.bank_free_inc(i, bk, u)
            self.ms(s, "s", "sc_ms", ("kcr", w, 1))

        @b.vector
        def _(v):
            self.wait_key(v, ("kcr", w, 1))
            for c in range(16):
                self.tt(v, "v", self.kcb[c][:, :], self.kcb[c][:, :],
                        self.kcb[c][:, :], OP.mult)
            self.ms(v, "v", "v_ms", ("kcq", w, 1))

        @b.sync
        def _(sy):
            self.wait_key(sy, ("rc", w))
            i = sy.dma_start(
                out=self.rc_sp.ap().rearrange("(c p) t -> p c t", p=128)[
                    :, :, w * W:(w + 1) * W],
                in_=self.rcs.ap().rearrange("p (c q) -> p c q", q=W))
            self.mark(i, "sy_rcsp", ("rcsp", w), by=16)
            self.wait_key(sy, ("kcq", w, 1))
            for c in range(16, self.KCH):
                i = sy.dma_start(
                    out=self.q_sp[c * 128:(c + 1) * 128, w * W:(w + 1) * W],
                    in_=self.kcb[c - 16][:, :])
                self.mark(i, "sy_qsp1", by=16)
            self.setev("sy_qsp1", ("qsp1", w))

    def emit_1bvv(self, sc):
        b = self.block
        NCH, KCH, C, W, NTT = self.NCH, self.KCH, self.C, self.W, self.NTT

        self.wv_b2 = self.sb(sc, "wv_b2", [128, KCH * C], BF16)
        self.qwin = self.sb(sc, "c_qwin", [128, KCH * W], BF16)
        self.x_tm = [self.sb(sc, f"c_xt{i}", [128, NTT * C], F32)
                     for i in range(2)]
        self.xn_tm = self.sb(sc, "c_x1b", [128, NTT * C], F32)
        self.rcb = self.sb(sc, "c_rc", [128, NCH * W], BF16)
        self.rwv = self.sb(sc, "c_rwv", [128, NCH * W], BF16)
        self.rw_tm = self.sb(sc, "c_rwtm", [128, NTT * C], BF16)

        @b.gpsimd
        def _(g):
            for cc in range(KCH):
                i = g.dma_start(out=self.wv_b2[:, cc * C:(cc + 1) * C],
                                in_=self.wvcT_d[cc * 128:(cc + 1) * 128, :])
                self.mark(i, "g_ld", by=16)
            self.setev("g_ld", ("wvc",))

        for w in range(self.NWIN):
            self.win_1bvv(w)

    def win_1bvv(self, w):
        b = self.block
        NCH, KCH, C, W, NTT = self.NCH, self.KCH, self.C, self.W, self.NTT
        buf = w % 2
        tr0 = w * W // 128

        @b.sync
        def _(sy):
            if w >= 1:
                self.wait_key(sy, ("vvdone", w - 1))
            i = sy.dma_start(
                out=self.qwin.ap().rearrange("p (c q) -> p c q", q=W),
                in_=self.q_sp.ap().rearrange("(c p) t -> p c t", p=128)[
                    :, :, w * W:(w + 1) * W])
            self.mark(i, "sy_qld", ("qld", w), by=16)
            if w >= 2:
                self.wait_key(sy, ("accsp", w - 2))
            i = sy.dma_start(
                out=self.x_tm[buf].ap().rearrange("p (t c) -> p t c", c=C),
                in_=self.x_d.ap().rearrange("(t p) c -> p t c", p=128)[
                    :, tr0:tr0 + NTT, :])
            self.mark(i, f"sy_x{buf}", ("xld2", w), by=16)
            if w >= 1:
                self.wait_key(sy, ("acc", w - 1))
            i = sy.dma_start(
                out=self.xn_tm.ap().rearrange("p (t c) -> p t c", c=C),
                in_=self.x1_sp.ap().rearrange("(t p) c -> p t c", p=128)[
                    :, tr0:tr0 + NTT, :])
            self.mark(i, "sy_x1l", ("x1ld2", w), by=16)
            if w >= 1:
                self.wait_key(sy, ("rcvall", w - 1))
            i = sy.dma_start(
                out=self.rcb.ap().rearrange("p (c q) -> p c q", q=W),
                in_=self.rc_sp.ap().rearrange("(c p) t -> p c t", p=128)[
                    :, :, w * W:(w + 1) * W])
            self.mark(i, "sy_rcl", ("rcld", w), by=16)

        @b.tensor
        def _(t):
            if w == 0:
                self.wait_key(t, ("wvc",))
            self.wait_key(t, ("qld", w))
            for c in range(NCH):
                pt, bk, u = self.take_bank(t, ev="v")
                for cc in range(KCH):
                    i = t.matmul(pt[:, 0:W],
                                 self.wv_b2[:, cc * C + c * 128:
                                            cc * C + (c + 1) * 128],
                                 self.qwin[:, cc * W:(cc + 1) * W],
                                 start=(cc == 0), stop=(cc == KCH - 1))
                self.mark(i, "t_mm", ("mm", bk, u))
                self.events[("vvbank", w, c)] = (bk, u)
            self.ms(t, "t", "t_ms", ("vvdone", w))

        @b.vector
        def _(v):
            self.wait_key(v, ("rcld", w))
            if w >= 1:
                self.wait_key(v, ("rcvTa", w - 1))
            for c in range(NCH):
                bk, u = self.events[("vvbank", w, c)]
                self.wait_key(v, ("mm", bk, u))
                pt = self.psb[bk]
                i = self.tt(v, "v", self.rwv[:, c * W:(c + 1) * W],
                            pt[:, 0:W], self.rcb[:, c * W:(c + 1) * W],
                            OP.mult)
                self.bank_free_inc(i, bk, u)
            self.ms(v, "v", "v_ms", ("rcvall", w))

        @b.scalar
        def _(s):
            self.wait_key(s, ("rcvall", w))
            if w >= 1:
                self.wait_key(s, ("acc", w - 1))
            for c in range(NCH):
                i = s.dma_start(
                    out=self.rw_tm.ap().rearrange(
                        "p (t cq) -> p t cq", cq=C)[
                        :, :, c * 128:(c + 1) * 128],
                    in_=self.rwv[:, c * W:(c + 1) * W].rearrange(
                        "p (t q) -> p t q", q=128),
                    transpose=True)
                self.mark(i, "sc_rwT0", by=16)
            self.setev("sc_rwT0", ("rcvTa", w))

        @b.vector
        def _(v):
            self.wait_key(v, ("rcvTa", w))
            self.wait_key(v, ("xld2", w))
            self.wait_key(v, ("x1ld2", w))
            for t in range(NTT):
                self.tt(v, "v", self.x_tm[buf][:, t * C:(t + 1) * C],
                        self.x_tm[buf][:, t * C:(t + 1) * C],
                        self.xn_tm[:, t * C:(t + 1) * C], OP.add)
            for t in range(NTT):
                self.tt(v, "v", self.x_tm[buf][:, t * C:(t + 1) * C],
                        self.x_tm[buf][:, t * C:(t + 1) * C],
                        self.rw_tm[:, t * C:(t + 1) * C], OP.add)
            self.ms(v, "v", "v_ms", ("acc", w))

        @b.sync
        def _(sy):
            self.wait_key(sy, ("acc", w))
            i = sy.dma_start(
                out=self.acc_sp.ap().rearrange("(t p) c -> p t c", p=128)[
                    :, tr0:tr0 + NTT, :],
                in_=self.x_tm[buf].ap().rearrange("p (t c) -> p t c", c=C))
            self.mark(i, f"sy_acsp{buf}", ("accsp", w), by=16)

    def emit_wq(self, pro):
        b = self.block
        C, IC = self.C, self.IC
        st = self.strips

        wqf = self.sb(pro, "wqf", [128, IC], F32)
        wqg = self.sb(pro, "wqg", [128, IC], F32)
        wqi = self.sb(pro, "wqi", [128, IC], I32)
        wqb = self.sb(pro, "wqb", [128, IC], BF16)
        wqt = self.sb(pro, "wqt", [128, IC], BF16)

        for name, wd, rows, kin, scs, spT in (
                ("g", self.wg_d, self.ICH, C, self.sc_g, self.wgT_sp),
                ("u", self.wu_d, self.ICH, C, self.sc_u, self.wuT_sp),
                ("d", self.wd_d, self.NCH, IC, self.sc_d, self.wdT_sp)):
            for r in range(rows):
                @b.sync
                def _(sy, r=r, wd=wd, kin=kin, name=name):
                    if not (name == "g" and r == 0):
                        self.wait_key(sy, ("wqv", self._prev_wq))
                    i = sy.dma_start(out=wqf[:, 0:kin],
                                     in_=wd[r * 128:(r + 1) * 128, :])
                    self.mark(i, "sy_wq", ("wqld", name, r), by=16)

                @b.vector
                def _(v, r=r, kin=kin, name=name, scs=scs):
                    self.wait_key(v, ("wqld", name, r))
                    wf = wqf[:, 0:kin]
                    self.red(v, "v", st[:, TMPA:TMPA + 1], wf, OP.max,
                             absval=True)
                    self.ts(v, "v", scs[:, r:r + 1], st[:, TMPA:TMPA + 1],
                            1.0 / QMAX, None, OP.mult)
                    self.recip(v, "v", st[:, TMPB:TMPB + 1], scs[:, r:r + 1])
                    self.ts(v, "v", wqg[:, 0:kin], wf, st[:, TMPB:TMPB + 1],
                            None, OP.mult)
                    self.cp(v, "v", wqi[:, 0:kin], wqg[:, 0:kin])
                    self.cp(v, "v", wqg[:, 0:kin], wqi[:, 0:kin])
                    self.ts(v, "v", wqb[:, 0:kin], wqg[:, 0:kin],
                            -(QMAX + 1.0), QMAX, OP.max, OP.min)
                    self.ms(v, "v", "v_ms", ("wqv", (name, r)))
                    self._prev_wq = (name, r)

                @b.scalar
                def _(s, r=r, kin=kin, name=name):
                    self.wait_key(s, ("wqv", (name, r)))
                    if not (name == "g" and r == 0):
                        self.wait_key(s, ("wqsp", self._prev_wqt))
                    i = s.dma_start(
                        out=wqt[:, 0:kin].rearrange("p (n q) -> p n q",
                                                         q=128),
                        in_=wqb[:, 0:kin].rearrange("p (n q) -> p n q",
                                                         q=128),
                        transpose=True)
                    self.mark(i, "sc_wqT", ("wqt", name, r), by=16)

                @b.sync
                def _(sy, r=r, kin=kin, name=name, spT=spT):
                    self.wait_key(sy, ("wqt", name, r))
                    i = sy.dma_start(
                        out=spT.ap().rearrange("(n p) q -> p n q", p=128)[
                            :, :, r * 128:(r + 1) * 128],
                        in_=wqt[:, 0:kin].rearrange("p (n q) -> p n q",
                                                         q=128))
                    self.mark(i, "sy_wqsp", ("wqsp", (name, r)), by=16)
                    self._prev_wqt = (name, r)
        self.setev("sy_wqsp", ("wqall",))

    def emit_pass2(self, sc):
        b = self.block
        NCH, ICH, C, IC, W, NTT = (self.NCH, self.ICH, self.C, self.IC,
                                   self.W, self.NTT)
        st = self.strips

        self.wg_b = self.sb(sc, "wg_b", [128, NCH * IC], BF16)
        self.wu_b = self.sb(sc, "wu_b", [128, NCH * IC], BF16)
        self.x_tm = [self.sb(sc, f"d_x{i}", [128, NTT * C], F32)
                     for i in range(2)]
        self.xqs = self.sb(sc, "d_s2", [128, NTT * C // 2], F32)
        self.xqi = self.sb(sc, "d_qi", [128, NTT * C // 2], I32)
        self.pre_bf = self.sb(sc, "d_pre", [128, NTT * C], BF16)
        self.pre_f = self.sb(sc, "d_pref", [128, NCH * W], BF16)
        self.sgb = self.sb(sc, "d_sg", [128, ICH * W], BF16)
        self.midb = [self.sb(sc, f"d_mid{i}", [128, W], F32)
                     for i in range(11)]
        self.amxb = self.sb(sc, "d_amx", [128, ICH], F32)
        if self.cfg.get("DBG"):
            self.dbgpb = self.sb(sc, "d_dbgpb", [128, W], F32)

        @b.sync
        def _(sy):
            self.wait_key(sy, ("wqall",))
            for cc in range(NCH):
                i = sy.dma_start(out=self.wg_b[:, cc * IC:(cc + 1) * IC],
                                 in_=self.wgT_sp[cc * 128:(cc + 1) * 128, :])
                self.mark(i, "sy_ld", by=16)
                i = sy.dma_start(out=self.wu_b[:, cc * IC:(cc + 1) * IC],
                                 in_=self.wuT_sp[cc * 128:(cc + 1) * 128, :])
                self.mark(i, "sy_ld", by=16)
            self.setev("sy_ld", ("wgu",))

        @b.vector
        def _(v):
            self.wait_key(v, ("scales", 1))
            self.ts(v, "v", self.cmb_g[:, :], self.sc_g[:, :],
                    st[:, SXV:SXV + 1], None, OP.mult)
            self.ts(v, "v", self.cmb_u[:, :], self.sc_u[:, :],
                    st[:, SXV:SXV + 1], None, OP.mult)
            self.ms(v, "v", "v_ms", ("cmb",))

        for w in range(self.NWIN):
            self.win_2(w)

        @b.vector
        def _(v):
            self.ms(v, "v", "v_ms", ("amax", 2))

    def win_2(self, w):
        b = self.block
        NCH, ICH, C, IC, W, NTT = (self.NCH, self.ICH, self.C, self.IC,
                                   self.W, self.NTT)
        buf = w % 2
        st = self.strips
        tr0 = w * W // 128

        @b.sync
        def _(sy):
            if w >= 2:
                self.wait_key(sy, ("xq", w - 2))
            i = sy.dma_start(
                out=self.x_tm[buf].ap().rearrange("p (t c) -> p t c", c=C),
                in_=self.xn_sp.ap().rearrange("(t p) c -> p t c", p=128)[
                    :, tr0:tr0 + NTT, :])
            self.mark(i, f"sy_x{buf}", ("xnld", w), by=16)

        @b.vector
        def _(v):
            self.wait_key(v, ("xnld", w))
            if w == 0:
                self.wait_key(v, ("wgu",))
                self.wait_key(v, ("cmb",))
            if w >= 1:
                self.wait_key(v, ("xqT", w - 1))
            if self.cfg.get("DBG") and w >= 1:
                self.wait_key(v, ("xqd", w - 1))
            xb = self.x_tm[buf]
            HF = NTT * C // 2
            for hh in range(2):
                sl = slice(hh * HF, (hh + 1) * HF)
                self.ts(v, "v", self.xqs[:, 0:HF], xb[:, sl],
                        st[:, RSX:RSX + 1], None, OP.mult)
                self.cp(v, "v", self.xqi[:, 0:HF], self.xqs[:, 0:HF])
                self.cp(v, "v", self.xqs[:, 0:HF], self.xqi[:, 0:HF])
                self.ts(v, "v", self.pre_bf[:, sl], self.xqs[:, 0:HF],
                        -(QMAX + 1.0), QMAX, OP.max, OP.min)
            self.ms(v, "v", "v_ms", ("xq", w))

        if self.cfg.get("DBG"):
            @b.sync
            def _(sy):
                self.wait_key(sy, ("xq", w))
                i = sy.dma_start(
                    out=self.xqd_sp.ap().rearrange("(t p) c -> p t c", p=128)[
                        :, tr0:tr0 + NTT, :],
                    in_=self.pre_bf.ap().rearrange("p (t c) -> p t c", c=C))
                self.mark(i, "sy_xqd", ("xqd", w), by=16)

        @b.scalar
        def _(s):
            self.wait_key(s, ("xq", w))
            if w >= 1:
                self.wait_key(s, ("gumm", w - 1))
            for t in range(NTT):
                i = s.dma_start(
                    out=self.pre_f.ap().rearrange(
                        "p (c q) -> p c q", q=W)[:, :, t * 128:(t + 1) * 128],
                    in_=self.pre_bf.ap().rearrange(
                        "p (t c q) -> p (t c) q", q=128, c=NCH)[
                        :, t * NCH:(t + 1) * NCH, :],
                    transpose=True)
                self.mark(i, "sc_preT", by=16)
            self.setev("sc_preT", ("xqT", w))

        if self.cfg.get("DBG") and w == 0:
            @b.sync
            def _(sy):
                self.wait_key(sy, ("xqT", 0))
                for cc in range(NCH):
                    i = sy.dma_start(
                        out=self.pf_sp[cc * 128:(cc + 1) * 128, :],
                        in_=self.pre_f[:, cc * W:(cc + 1) * W])
                    self.mark(i, "sy_pfd", by=16)
                self.setev("sy_pfd", ("pfd",))

        @b.tensor
        def _(t):
            self.wait_key(t, ("xqT", w))
            if self.cfg.get("DBG") and w == 1:
                self.wait_key(t, ("pfd",))
            for q, wb in ((0, self.wg_b), (1, self.wu_b)):
                for c in range(ICH):
                    pt, bk, u = self.take_bank(t, ev="s" if q == 0 else "v")
                    for cc in range(NCH):
                        i = t.matmul(pt[:, 0:W],
                                     wb[:, cc * IC + c * 128:
                                        cc * IC + (c + 1) * 128],
                                     self.pre_f[:, cc * W:(cc + 1) * W],
                                     start=(cc == 0), stop=(cc == NCH - 1))
                    self.mark(i, "t_mm", ("mm", bk, u))
                    self.events[("gub", w, c, q)] = (bk, u)
            self.ms(t, "t", "t_ms", ("gumm", w))

        if self.cfg.get("DBG") and w == 0:
            @b.vector
            def _(v):
                bk, u = self.events[("gub", 0, 0, 0)]
                self.wait_key(v, ("mm", bk, u))
                self.cp(v, "v", self.dbgpb[:, :], self.psb[bk][:, 0:W])
                self.ms(v, "v", "v_ms", ("pbcp",))

            @b.sync
            def _(sy):
                self.wait_key(sy, ("pbcp",))
                i = sy.dma_start(out=self.pb_sp[:, :], in_=self.dbgpb[:, :])
                self.mark(i, "sy_pbd", ("pbd",), by=16)
                i = sy.dma_start(out=self.pc_sp[:, 0:ICH],
                                 in_=self.cmb_g[:, :])
                self.mark(i, "sy_pbd", by=16)
                i = sy.dma_start(out=self.pc_sp[:, 32:32 + ICH],
                                 in_=self.cmb_u[:, :])
                self.mark(i, "sy_pbd", by=16)

        @b.scalar
        def _(s):
            self.pin_act(s, set_id=18)
            if self.cfg.get("DBG") and w >= 1:
                self.wait_key(s, ("sgd", w - 1))
            if w >= 1:
                self.wait_key(s, ("midall", w - 1))
            for c in range(ICH):
                bk, u = self.events[("gub", w, c, 0)]
                self.wait_key(s, ("mm", bk, u))
                if self.cfg.get("DBG") and w == 0 and c == 0:
                    self.wait_key(s, ("pbcp",))
                pt = self.psb[bk]
                i = self.act(s, "s", self.sgb[:, c * W:(c + 1) * W],
                             pt[:, 0:W], AF.Silu, scale=self.cmb_g[:, c:c + 1])
                self.bank_free_inc(i, bk, u)
            self.ms(s, "s", "sc_ms", ("sgall", w))

        if self.cfg.get("DBG"):
            @b.sync
            def _(sy):
                self.wait_key(sy, ("sgall", w))
                for c in range(ICH):
                    i = sy.dma_start(
                        out=self.sg_sp[c * 128:(c + 1) * 128,
                                       w * W:(w + 1) * W],
                        in_=self.sgb[:, c * W:(c + 1) * W])
                    self.mark(i, "sy_sgd", by=16)
                self.setev("sy_sgd", ("sgd", w))

        def mid_chunk(v, c):
            gbk, gu = self.events[("gub", w, c, 0)]
            self.wait_key(v, ("bkfree", gbk, gu))
            bk, u = self.events[("gub", w, c, 1)]
            self.wait_key(v, ("mm", bk, u))
            pt = self.psb[bk]
            i = self.stt(v, "v", self.midb[c % 11][:, :], pt[:, 0:W],
                         self.cmb_u[:, c:c + 1], self.sgb[:, c * W:(c + 1) * W],
                         OP.mult, OP.mult)
            self.bank_free_inc(i, bk, u)
            self.red(v, "v", self.amxb[:, c:c + 1], self.midb[c % 11][:, :],
                     OP.max, absval=True)

        @b.vector
        def _(v):
            if w >= 1:
                self.wait_key(v, ("misp1", w - 1))
            for c in range(0, 11):
                mid_chunk(v, c)
            self.ms(v, "v", "v_ms", ("mid0", w))

        @b.sync
        def _(sy):
            self.wait_key(sy, ("mid0", w))
            for c in range(0, 11):
                i = sy.dma_start(
                    out=self.mid_sp[c * 128:(c + 1) * 128,
                                    w * W:(w + 1) * W],
                    in_=self.midb[c % 11][:, :])
                self.mark(i, "sy_misp0", by=16)
            self.setev("sy_misp0", ("misp0", w))

        @b.vector
        def _(v):
            self.wait_key(v, ("misp0", w))
            for c in range(11, ICH):
                mid_chunk(v, c)
            self.red(v, "v", st[:, TMPA:TMPA + 1], self.amxb[:, 0:ICH],
                     OP.max)
            if w == 0:
                self.cp(v, "v", st[:, AM2:AM2 + 1], st[:, TMPA:TMPA + 1])
            else:
                self.tt(v, "v", st[:, AM2:AM2 + 1], st[:, AM2:AM2 + 1],
                        st[:, TMPA:TMPA + 1], OP.max)
            self.ms(v, "v", "v_ms", ("mid1", w))
            self.ms(v, "v", "v_ms", ("midall", w))

        @b.sync
        def _(sy):
            self.wait_key(sy, ("mid1", w))
            for c in range(11, ICH):
                i = sy.dma_start(
                    out=self.mid_sp[c * 128:(c + 1) * 128,
                                    w * W:(w + 1) * W],
                    in_=self.midb[c % 11][:, :])
                self.mark(i, "sy_misp1", by=16)
            self.setev("sy_misp1", ("misp1", w))

    def emit_pass3(self, sc):
        b = self.block
        NCH, ICH, C, W, NTT = self.NCH, self.ICH, self.C, self.W, self.NTT
        st = self.strips

        self.wd_b = self.sb(sc, "wd_b", [128, ICH * C], BF16)
        self.x_tm = [self.sb(sc, f"e_x{i}", [128, NTT * C], F32)
                     for i in range(2)]
        self.mldbB = self.sb(sc, "e_ml", [128, 8 * W], F32)
        self.qsf = self.sb(sc, "e_qf", [128, 4 * W], F32)
        self.qsi = self.sb(sc, "e_qi", [128, 4 * W], I32)
        self.hqw = [self.sb(sc, f"e_hq{i}", [128, ICH * W], BF16)
                    for i in range(2)]
        self.rw_bf = [self.sb(sc, f"e_rw{i}", [128, W], BF16)
                      for i in range(2)]
        self.rw_tm = self.sb(sc, "e_rwtm", [128, NTT * C], BF16)

        @b.sync
        def _(sy):
            for cc in range(ICH):
                i = sy.dma_start(out=self.wd_b[:, cc * C:(cc + 1) * C],
                                 in_=self.wdT_sp[cc * 128:(cc + 1) * 128, :])
                self.mark(i, "sy_ld", by=16)
            self.setev("sy_ld", ("wdT",))

        @b.vector
        def _(v):
            self.wait_key(v, ("scales", 2))
            self.ts(v, "v", self.cmb_d[:, :], self.sc_d[:, :],
                    st[:, SHV:SHV + 1], None, OP.mult)
            self.ms(v, "v", "v_ms", ("cmbd",))

        for w in range(self.NWIN):
            self.win_3(w)

    def win_3(self, w):
        b = self.block
        NCH, ICH, C, W, NTT = self.NCH, self.ICH, self.C, self.W, self.NTT
        buf = w % 2
        st = self.strips
        tr0 = w * W // 128

        GW = 4
        groups = [(g0, min(g0 + GW, ICH)) for g0 in range(0, ICH, GW)]
        NG = len(groups)
        for gi, (c0, c1) in enumerate(groups):
            @b.sync
            def _(sy, gi=gi, c0=c0, c1=c1):
                if gi >= 2:
                    self.wait_key(sy, ("hqg", w, gi - 2))
                elif w >= 1:
                    self.wait_key(sy, ("hqg", w - 1, NG - 2 + gi))
                pb = (gi % 2) * GW
                for c in range(c0, c1):
                    par = c % 8
                    i = sy.dma_start(
                        out=self.mldbB[:, (pb + c - c0) * W:
                                       (pb + c - c0 + 1) * W],
                        in_=self.mid_sp[c * 128:(c + 1) * 128,
                                        w * W:(w + 1) * W])
                    self.mark(i, f"sy_ml{par}", ("mld", w, c), by=16)

            @b.vector
            def _(v, gi=gi, c0=c0, c1=c1):
                if gi == 0:
                    if w == 0:
                        self.wait_key(v, ("cmbd",))
                    if w >= 2:
                        self.wait_key(v, ("dmm", w - 2))
                hq = self.hqw[w % 2]
                for c in range(c0, c1):
                    self.wait_key(v, ("mld", w, c))
                pb = (gi % 2) * GW
                gw = c1 - c0
                sl = slice(pb * W, (pb + gw) * W)
                self.ts(v, "v", self.qsf[:, 0:gw * W], self.mldbB[:, sl],
                        st[:, RSH:RSH + 1], None, OP.mult)
                self.cp(v, "v", self.qsi[:, 0:gw * W], self.qsf[:, 0:gw * W])
                self.cp(v, "v", self.qsf[:, 0:gw * W], self.qsi[:, 0:gw * W])
                self.ts(v, "v", hq[:, c0 * W:c1 * W], self.qsf[:, 0:gw * W],
                        -(QMAX + 1.0), QMAX, OP.max, OP.min)
                self.ms(v, "v", "v_ms", ("hqg", w, gi))

        for co in range(NCH):
            @b.tensor
            def _(t, co=co):
                if co == 0 and w == 0:
                    self.wait_key(t, ("wdT",))
                hq = self.hqw[w % 2]
                pt, bk, u = self.take_bank(t, ev="v")
                for c in range(ICH):
                    if co == 0 and c % 4 == 0:
                        self.wait_key(t, ("hqg", w, c // 4))
                    i = t.matmul(pt[:, 0:W],
                                 self.wd_b[:, c * C + co * 128:
                                           c * C + (co + 1) * 128],
                                 hq[:, c * W:(c + 1) * W],
                                 start=(c == 0), stop=(c == ICH - 1))
                self.mark(i, "t_mm", ("mm", bk, u))
                self.events[("dwb", w, co)] = (bk, u)
                if co == NCH - 1:
                    self.ms(t, "t", "t_ms", ("dmm", w))

            @b.vector
            def _(v, co=co):
                if co >= 2:
                    self.wait_key(v, ("dwT", w, co - 2))
                bk, u = self.events[("dwb", w, co)]
                self.wait_key(v, ("mm", bk, u))
                pt = self.psb[bk]
                i = self.ts(v, "v", self.rw_bf[co % 2][:, :], pt[:, 0:W],
                            self.cmb_d[:, co:co + 1], None, OP.mult)
                self.bank_free_inc(i, bk, u)
                self.ms(v, "v", "v_ms", ("dwn", w, co))

            @b.scalar
            def _(s, co=co):
                self.wait_key(s, ("dwn", w, co))
                if co == 0 and w >= 1:
                    self.wait_key(s, ("outv", w - 1))
                i = s.dma_start(
                    out=self.rw_tm.ap().rearrange(
                        "p (t cq) -> p t cq", cq=C)[
                        :, :, co * 128:(co + 1) * 128],
                    in_=self.rw_bf[co % 2].ap().rearrange("p (t q) -> p t q",
                                                          q=128),
                    transpose=True)
                self.mark(i, f"sc_rwT{co % 2}", ("dwT", w, co), by=16)

        @b.sync
        def _(sy):
            if w >= 2:
                self.wait_key(sy, ("outw", w - 2))
            i = sy.dma_start(
                out=self.x_tm[buf].ap().rearrange("p (t c) -> p t c", c=C),
                in_=self.acc_sp.ap().rearrange("(t p) c -> p t c", p=128)[
                    :, tr0:tr0 + NTT, :])
            self.mark(i, f"sy_x{buf}", ("accld", w), by=16)

        @b.vector
        def _(v):
            self.wait_key(v, ("accld", w))
            self.wait_key(v, ("dwT", w, NCH - 1))
            self.wait_key(v, ("dwT", w, NCH - 2))
            for t in range(NTT):
                self.tt(v, "v", self.x_tm[buf][:, t * C:(t + 1) * C],
                        self.x_tm[buf][:, t * C:(t + 1) * C],
                        self.rw_tm[:, t * C:(t + 1) * C], OP.add)
            self.ms(v, "v", "v_ms", ("outv", w))

        @b.sync
        def _(sy):
            self.wait_key(sy, ("outv", w))
            i = sy.dma_start(
                out=self.out_d.ap().rearrange("(t p) c -> p t c", p=128)[
                    :, tr0:tr0 + NTT, :],
                in_=self.x_tm[buf].ap().rearrange("p (t c) -> p t c", c=C))
            self.mark(i, f"sy_out{w % 2}", ("outw", w), by=16)
            if w == self.NWIN - 1:
                if self.cfg.get("DBG"):
                    i = sy.dma_start(out=self.dbg_sp[:, :],
                                     in_=self.strips[:, :])
                    self.mark(i, "sy_out0", by=16)
                self.wait_all(sy, "sy_out0")
                self.wait_all(sy, "sy_out1")


def build_nc(cfg=None):
    cfg = dict(DEF_CFG, **(cfg or {}))
    return Builder(cfg).build()


def prep_inputs(inputs, cfg=None):
    cfg = dict(DEF_CFG, **(cfg or {}))
    NSEQ, T, C, NCORE = cfg["NSEQ"], cfg["T"], cfg["C"], cfg["NCORE"]
    x = np.ascontiguousarray(np.asarray(inputs["L0_attn_x"], np.float32))
    B = x.shape[0]
    _vl = [
        np.asarray(inputs["L0_attn_ln1_g"], np.float32),
        np.asarray(inputs["L0_attn_ln1_b"], np.float32),
        np.asarray(inputs["L0_attn_ln2_g"], np.float32),
        np.asarray(inputs["L0_attn_ln2_b"], np.float32),
        np.asarray(inputs["L0_attn_mix_k"], np.float32),
        np.asarray(inputs["L0_attn_mix_v"], np.float32),
        np.asarray(inputs["L0_attn_mix_r"], np.float32),
        np.asarray(inputs["L0_attn_cmix_k"], np.float32),
        np.asarray(inputs["L0_attn_cmix_r"], np.float32),
        np.asarray(inputs["L0_attn_time_first"], np.float32),
    ]
    NCH = C // 128
    vecs = np.ascontiguousarray(
        np.stack(_vl, axis=0).reshape(10, NCH, 128)
        .transpose(2, 0, 1).reshape(128, 10 * NCH))
    clips = np.ascontiguousarray(np.array([[
        float(np.asarray(inputs["L0_ffn_clip_x"]).reshape(-1)[0]),
        float(np.asarray(inputs["L0_ffn_clip_down"]).reshape(-1)[0])]],
        np.float32))
    shared = {
        "vecs": vecs, "clips": clips,
        "wkT": np.ascontiguousarray(
            np.asarray(inputs["L0_attn_Wk_t"], np.float32).T),
        "wvT": np.ascontiguousarray(
            np.asarray(inputs["L0_attn_Wv_t"], np.float32).T),
        "wrT": np.ascontiguousarray(
            np.asarray(inputs["L0_attn_Wr_t"], np.float32).T),
        "wkcT": np.ascontiguousarray(
            np.asarray(inputs["L0_attn_Wk_c"], np.float32).T),
        "wvcT": np.ascontiguousarray(
            np.asarray(inputs["L0_attn_Wv_c"], np.float32).T),
        "wrcT": np.ascontiguousarray(
            np.asarray(inputs["L0_attn_Wr_c"], np.float32).T),
        "wg": np.ascontiguousarray(
            np.asarray(inputs["L0_ffn_w_gate"], np.float32)),
        "wu": np.ascontiguousarray(
            np.asarray(inputs["L0_ffn_w_up"], np.float32)),
        "wd": np.ascontiguousarray(
            np.asarray(inputs["L0_ffn_w_down"], np.float32)),
    }
    per = B // NCORE
    in_maps = []
    for i in range(NCORE):
        m = dict(shared)
        m["x"] = np.ascontiguousarray(
            x[i * per:(i + 1) * per].reshape(per * T, C))
        in_maps.append(m)
    return in_maps, per


def kernel(**inputs):
    cfg = dict(DEF_CFG)
    in_maps, per = prep_inputs(inputs, cfg)
    nc = build_nc(cfg)
    res = run_bass_kernel_spmd(nc, in_maps,
                               core_ids=list(range(cfg["NCORE"])))
    T, C = cfg["T"], cfg["C"]
    out = np.concatenate(
        [np.asarray(r["out"]).reshape(per, T, C) for r in res.results],
        axis=0)
    return out


# version 16
# speedup vs baseline: 1.0263x; 1.0263x over previous
import numpy as np
from contextlib import ExitStack

import concourse.bass as bass
import concourse.mybir as mybir
from concourse.bass_utils import run_bass_kernel_spmd

F32 = mybir.dt.float32
BF16 = mybir.dt.bfloat16
I32 = mybir.dt.int32
AF = mybir.ActivationFunctionType
OP = mybir.AluOpType
AX = mybir.AxisListType.X

EPS_LN = 1e-5
EPS_WKV = 1e-8
QMAX = 127.0

DEF_CFG = dict(NSEQ=4, T=2048, W=512, C=1024, IC=2816, KC=4096, NCORE=8,
               EPS_EXACT=True, DBG=False)

MU0, RS0 = 0, 4
MU1, RS1 = 8, 12
AM1, AM2 = 16, 17
RSX, RSH = 18, 19
SXV, SHV = 20, 21
TMPA, TMPB = 22, 23
SIGT = 24
CLIPX, CLIPD = 26, 27
CEPS = 28


class Builder:
    def __init__(self, cfg):
        self.cfg = cfg
        self.NSEQ = cfg["NSEQ"]; self.T = cfg["T"]; self.W = cfg["W"]
        self.C = cfg["C"]; self.IC = cfg["IC"]; self.KC = cfg["KC"]
        self.NCH = self.C // 128
        self.ICH = self.IC // 128
        self.KCH = self.KC // 128
        self.WPS = self.T // self.W
        self.NWIN = self.NSEQ * self.WPS
        self.NTT = self.W // 128
        self.NTOK = self.NSEQ * self.T
        self.EPS_EXACT = cfg["EPS_EXACT"]
        assert self.W <= 512
        self.events = {}
        self.es = ExitStack()
        self.nc = bass.Bass("TRN2", target_bir_lowering=False)
        self.dirty = {}
        self.sems = {}
        self.sem_n = {}
        self.nbar = 0

    def sem(self, name):
        if name not in self.sems:
            self.sems[name] = self.es.enter_context(self.nc.semaphore(name))
            self.sem_n[name] = 0
        return self.sems[name]

    def mark(self, instr, semname, key=None, by=1):
        h = self.sem(semname)
        instr.then_inc(h, by)
        self.sem_n[semname] += by
        if key is not None:
            self.events[key] = (semname, self.sem_n[semname])
        return self.sem_n[semname]

    def pin_act(self, s, set_id=6):
        s.add_instruction(mybir.InstLoadActFuncSet(
            name=self.nc.get_next_instruction_name(),
            act_func_set_id=set_id, ins=[], outs=[]))

    def ms(self, e, en, semname, key):
        e.drain()
        del self.dirty.setdefault(en, [])[:]
        del self.dirty.setdefault(en + "~r", [])[:]
        return self.mark(e.nop(), semname, key)

    def setev(self, semname, key):
        self.sem(semname)
        self.events[key] = (semname, self.sem_n[semname])

    def wait_key(self, eng, key):
        semname, val = self.events[key]
        if val > 0:
            eng.wait_ge(self.sems[semname], val)

    def wait_all(self, eng, semname):
        if semname in self.sems and self.sem_n[semname] > 0:
            eng.wait_ge(self.sems[semname], self.sem_n[semname])

    def sb(self, scope, name, shape, dt):
        return scope.enter_context(self.nc.sbuf_tensor(name, shape, dt))

    @staticmethod
    def _nm(x):
        t = getattr(x, "tensor", None)
        if t is not None:
            return t.name
        return getattr(x, "name", None)

    def _ivs(self, aps):
        out = []
        for a in aps:
            if a is None or isinstance(a, (int, float)):
                continue
            t = getattr(a, "tensor", None)
            if t is not None:
                try:
                    dims = [tuple(p) for p in a.ap]
                    off = int(a.offset)
                    if len(dims) >= 2:
                        span = sum((int(n) - 1) * int(s)
                                   for s, n in dims[1:]) + 1
                    else:
                        span = (int(dims[0][1]) - 1) * int(dims[0][0]) + 1
                    out.append((t.name, off, off + span))
                    continue
                except Exception:
                    out.append((t.name, 0, 1 << 40))
                    continue
            n = self._nm(a)
            if n:
                out.append((n, 0, 1 << 40))
        return out

    @staticmethod
    def _ov(ivs, st):
        for n, lo, hi in ivs:
            for n2, lo2, hi2 in st:
                if n == n2 and lo < hi2 and lo2 < hi:
                    return True
        return False

    def dep2(self, e, en, ins, outs):
        wr = self.dirty.setdefault(en, [])
        rd = self.dirty.setdefault(en + "~r", [])
        i_n = self._ivs(ins)
        o_n = self._ivs(outs)
        if self._ov(i_n, wr) or self._ov(o_n, wr) or self._ov(o_n, rd):
            e.drain()
            del wr[:]
            del rd[:]
        rd.extend(i_n)
        wr.extend(o_n)

    def dep(self, e, en, *aps):
        d = self.dirty.setdefault(en, [])
        if self._ov(self._ivs(aps), d):
            e.drain()
            del d[:]

    def wrote(self, en, *aps):
        d = self.dirty.setdefault(en, [])
        d.extend(self._ivs(aps))

    def tt(self, e, en, out, a, b, op):
        self.dep2(e, en, (a, b), (out,))
        i = e.tensor_tensor(out, a, b, op)
        return i

    def ts(self, e, en, out, a, s1, s2, op0, op1=OP.bypass, accum_out=None):
        self.dep2(e, en, (a, s1, s2), (out, accum_out))
        i = e.tensor_scalar(out, a, s1, s2, op0, op1, accum_out=accum_out)
        return i

    def stt(self, e, en, out, a, s, b, op0, op1, accum_out=None):
        self.dep2(e, en, (a, s, b), (out, accum_out))
        i = e.scalar_tensor_tensor(out, a, s, b, op0, op1,
                                   accum_out=accum_out)
        return i

    def cp(self, e, en, out, a):
        self.dep2(e, en, (a,), (out,))
        if hasattr(e, "tensor_copy"):
            i = e.tensor_copy(out, a)
        else:
            i = e.copy(out, a)
        return i

    def scan(self, e, en, out, d0, d1, init, op0, op1):
        self.dep2(e, en, (d0, d1, init), (out,))
        i = e.tensor_tensor_scan(out, d0, d1, init, op0, op1)
        return i

    def red(self, e, en, out, a, op, absval=False):
        self.dep2(e, en, (a,), (out,))
        i = e.tensor_reduce(out, a, AX, op, apply_absolute_value=absval)
        return i

    def act(self, e, en, out, a, fn, bias=0.0, scale=1.0, accum_out=None):
        self.dep2(e, en, (a, bias, scale), (out, accum_out))
        i = e.activation(out, a, fn, bias=bias, scale=scale,
                         accum_out=accum_out)
        return i

    def mset(self, e, en, out, v):
        self.dep2(e, en, (), (out,))
        i = e.memset(out, v)
        return i

    def recip(self, e, en, out, a):
        self.dep2(e, en, (a,), (out,))
        i = e.reciprocal(out, a)
        return i

    def take_bank(self, t, ev="s"):
        b = self.bank_ptr
        self.bank_ptr = (self.bank_ptr + 1) % self.NB
        u = self.bank_use[b]
        if u > 0:
            self.wait_key(t, ("bkfree", b, u - 1))
        self.bank_use[b] = u + 1
        semname = "sc_bk" if ev == "s" else "v_bk"
        h = self.sem(semname)
        self.bkcnt = getattr(self, "bkcnt", {})
        self.bkcnt[semname] = self.bkcnt.get(semname, 0) + 1
        self.events[("bkfree", b, u)] = (semname, self.bkcnt[semname])
        return self.psb[b], b, u

    def bank_free_inc(self, instr, b, u):
        semname, val = self.events[("bkfree", b, u)]
        instr.then_inc(self.sems[semname], 1)
        self.sem_n[semname] += 1
        assert self.sem_n[semname] == val, (semname, self.sem_n[semname], val)

    def barrier(self):
        idx = self.nbar
        self.nbar += 1
        b = self.block
        name = f"bar{idx}"
        h = self.sem(name)
        pref = {"sy": ("sy_",), "v": ("v_",), "s": ("sc_",), "g": ("g_",),
                "t": ("t_",)}

        def mk(engname):
            def _(e):
                for sn in list(self.sems):
                    if sn.startswith(pref[engname]):
                        self.wait_all(e, sn)
                e.nop().then_inc(h, 1)
                e.wait_ge(h, 5)
                del self.dirty.setdefault(engname, [])[:]
                del self.dirty.setdefault(engname + "~r", [])[:]
            return _
        b.sync(mk("sy")); b.vector(mk("v")); b.scalar(mk("s"))
        b.gpsimd(mk("g")); b.tensor(mk("t"))
        self.sem_n[name] = 5

    def build(self):
        nc, es = self.nc, self.es
        C, IC, KC, NTOK = self.C, self.IC, self.KC, self.NTOK

        self.x_d = nc.dram_tensor("x", [NTOK, C], F32, kind="ExternalInput")
        self.out_d = nc.dram_tensor("out", [NTOK, C], F32,
                                    kind="ExternalOutput")
        self.wkT_d = nc.dram_tensor("wkT", [C, C], F32, kind="ExternalInput")
        self.wvT_d = nc.dram_tensor("wvT", [C, C], F32, kind="ExternalInput")
        self.wrT_d = nc.dram_tensor("wrT", [C, C], F32, kind="ExternalInput")
        self.wkcT_d = nc.dram_tensor("wkcT", [C, KC], F32,
                                     kind="ExternalInput")
        self.wvcT_d = nc.dram_tensor("wvcT", [KC, C], F32,
                                     kind="ExternalInput")
        self.wrcT_d = nc.dram_tensor("wrcT", [C, C], F32,
                                     kind="ExternalInput")
        self.wg_d = nc.dram_tensor("wg", [IC, C], F32, kind="ExternalInput")
        self.wu_d = nc.dram_tensor("wu", [IC, C], F32, kind="ExternalInput")
        self.wd_d = nc.dram_tensor("wd", [C, IC], F32, kind="ExternalInput")
        self.vecs_d = nc.dram_tensor("vecs", [128, 10 * self.NCH], F32,
                                     kind="ExternalInput")
        self.clips_d = nc.dram_tensor("clips", [1, 2], F32,
                                      kind="ExternalInput")

        kd = "ExternalOutput" if self.cfg.get("DBG") else "Internal"
        self.xn_sp = nc.dram_tensor("xn_sp", [NTOK, C], F32, kind=kd)
        self.x1_sp = nc.dram_tensor("x1_sp", [NTOK, C], F32, kind=kd)
        self.acc_sp = nc.dram_tensor("acc_sp", [NTOK, C], F32, kind=kd)
        self.q_sp = nc.dram_tensor("q_sp", [KC, NTOK], BF16, kind=kd)
        self.rc_sp = nc.dram_tensor("rc_sp", [C, NTOK], BF16, kind=kd)
        self.mid_sp = nc.dram_tensor("mid_sp", [IC, NTOK], F32, kind=kd)
        self.wgT_sp = nc.dram_tensor("wgT_sp", [C, IC], BF16, kind=kd)
        self.wuT_sp = nc.dram_tensor("wuT_sp", [C, IC], BF16, kind=kd)
        self.wdT_sp = nc.dram_tensor("wdT_sp", [IC, C], BF16, kind=kd)
        self.am_sp = nc.dram_tensor("am_sp", [1, 128], F32)
        self.sc_sp = nc.dram_tensor("sc_sp", [1, 2], F32)
        self.dbg_sp = nc.dram_tensor("dbg_sp", [128, 32], F32, kind=kd)
        self.hq_sp = nc.dram_tensor("hq_sp", [IC, NTOK], BF16, kind=kd)
        self.dwn_sp = nc.dram_tensor("dwn_sp", [C, NTOK], BF16, kind=kd)
        self.xqd_sp = nc.dram_tensor("xqd_sp", [NTOK, C], BF16, kind=kd)
        self.pf_sp = nc.dram_tensor("pf_sp", [C, self.W], BF16, kind=kd)
        self.pb_sp = nc.dram_tensor("pb_sp", [128, self.W], F32, kind=kd)
        self.pc_sp = nc.dram_tensor("pc_sp", [128, 64], F32, kind=kd)
        self.sg_sp = nc.dram_tensor("sg_sp", [IC, NTOK], BF16, kind=kd)
        self.cc1i = nc.dram_tensor("cc1i", [1, 8], F32)
        self.cc1o = nc.dram_tensor("cc1o", [1, 8], F32)
        self.cc2i = nc.dram_tensor("cc2i", [1, 8], F32)
        self.cc2o = nc.dram_tensor("cc2o", [1, 8], F32)

        NCH, W = self.NCH, self.W
        self.strips = self.sb(es, "strips", [128, 32], F32)
        self.vcs = self.sb(es, "vcs", [128, 10 * NCH], F32)
        self.eu = self.sb(es, "eu", [128, NCH], F32)
        self.carP = self.sb(es, "carP", [128, NCH], F32)
        self.carA = self.sb(es, "carA", [128, NCH], F32)
        self.sc_g = self.sb(es, "sc_g", [128, self.ICH], F32)
        self.sc_u = self.sb(es, "sc_u", [128, self.ICH], F32)
        self.sc_d = self.sb(es, "sc_d", [128, NCH], F32)
        self.cmb_g = self.sb(es, "cmb_g", [128, self.ICH], F32)
        self.cmb_u = self.sb(es, "cmb_u", [128, self.ICH], F32)
        self.cmb_d = self.sb(es, "cmb_d", [128, NCH], F32)
        self.ccb = self.sb(es, "ccb", [128, 8], F32)
        self.amrow = self.sb(es, "amrow", [1, 128], F32)

        self.NB = 8
        self.psb = [es.enter_context(self.nc.psum_tensor(f"psb{i}",
                                                         [128, 512], F32))
                    for i in range(self.NB)]
        self.bank_use = [0] * self.NB
        self.bank_ptr = 0

        self.block = es.enter_context(nc.Block())

        with ExitStack() as sc:
            self.emit_setup_and_1a(sc)
            self.emit_amax_collective(1)
            self.barrier()
        with ExitStack() as sc:
            self.emit_1bkc(sc)
            self.barrier()
        with ExitStack() as sc:
            self.emit_1bvv(sc)
            self.barrier()
        with ExitStack() as sc:
            with ExitStack() as pro:
                self.emit_wq(pro)
            self.emit_pass2(sc)
            self.emit_amax_collective(2)
            self.barrier()
        with ExitStack() as sc:
            self.emit_pass3(sc)

        es.close()
        return nc

    def emit_setup_and_1a(self, sc):
        b = self.block
        NCH, C, W, NTT, WPS = self.NCH, self.C, self.W, self.NTT, self.WPS

        self.onesb = self.sb(sc, "onesb", [128, W], BF16)
        self.wk_b = self.sb(sc, "wk_b", [128, NCH * C], BF16)
        self.wv_b = self.sb(sc, "wv_b", [128, NCH * C], BF16)
        self.wr_b = self.sb(sc, "wr_b", [128, NCH * C], BF16)
        self.x_tm = [self.sb(sc, f"a_x{i}", [128, NTT * C], F32)
                     for i in range(2)]
        self.scrbf = self.sb(sc, "a_scrbf", [128, NCH * (W + 1)], BF16)
        self.pre_f = self.sb(sc, "a_pref", [128, NCH * W], BF16)
        self.tsk = self.sb(sc, "a_tsk", [128, NCH * W], BF16)
        self.tsv = self.sb(sc, "a_tsv", [128, NCH * W], BF16)
        self.tsr = self.sb(sc, "a_tsr", [128, NCH * W], BF16)
        H = NCH // 2
        self.H = H
        self.ek_w = self.sb(sc, "a_ekw", [128, NCH * W], BF16)
        self.v_w = self.sb(sc, "a_vw", [128, NCH * W], BF16)
        self.er_h = self.sb(sc, "a_erh", [128, H * W], BF16)
        self.wP_h = self.sb(sc, "a_wPh", [128, H * (W + 1)], BF16)
        self.wA = self.sb(sc, "a_wA", [128, H * (W + 1)], F32)
        self.wb = [self.sb(sc, f"a_wb{i}", [128, H * W], BF16)
                   for i in range(2)]
        self.wf = [self.sb(sc, f"a_wf{i}", [128, H * W], F32)
                   for i in range(2)]
        self.amxa = self.sb(sc, "a_amxa", [128, 8], F32)
        self.LNW = self.sb(sc, "a_LNW", [128, H * W], F32)
        self.LND = self.sb(sc, "a_LND", [128, H * W], F32)
        self.rw_h = self.sb(sc, "a_rwh", [128, H * W], BF16)
        self.hcar = self.sb(sc, "a_hcar", [128, NCH], F32)

        @b.gpsimd
        def _(g):
            for wd, wbuf in ((self.wkT_d, self.wk_b), (self.wvT_d, self.wv_b),
                             (self.wrT_d, self.wr_b)):
                for cc in range(NCH):
                    i = g.dma_start(out=wbuf[:, cc * C:(cc + 1) * C],
                                    in_=wd[cc * 128:(cc + 1) * 128, :])
                    self.mark(i, "g_ld", by=16)
            self.setev("g_ld", ("wproj",))

        @b.sync
        def _(sy):
            i = sy.dma_start(out=self.vcs[:, :], in_=self.vecs_d[:, :])
            self.mark(i, "sy_ld", by=16)
            i = sy.dma_start(out=self.strips[0:1, CLIPX:CLIPD + 1],
                             in_=self.clips_d[0:1, 0:2])
            self.mark(i, "sy_ld", by=16)
            self.setev("sy_ld", ("vecs",))
            self.setev("sy_ld", ("clips",))

        @b.vector
        def _(v):
            self.mset(v, "v", self.onesb[:, :], 1.0)
            self.mset(v, "v", self.carP[:, :], 0.0)
            self.mset(v, "v", self.carA[:, :], 0.0)
            self.mset(v, "v", self.strips[:, CEPS:CEPS + 1], EPS_LN)
            self.mset(v, "v", self.ccb[:, :], 0.0)
            self.mark(v.nop(), "v_ms", ("setupv",))

        @b.scalar
        def _(s):
            self.pin_act(s)
            self.wait_key(s, ("vecs",))
            self.act(s, "s", self.eu[:, :],
                     self.vcs[:, 9 * NCH:10 * NCH], AF.Exp)
            self.mark(s.nop(), "sc_ms", ("setups",))

        for w in range(self.NWIN):
            self.win_1a(w)

    def ln_stats_v(self, v, src, dummy, mu0, rs0):
        st, C, NTT = self.strips, self.C, self.NTT
        LT = (TMPA, TMPB, 25, 29)
        for t in range(NTT):
            self.ts(v, "v", dummy[:, t * C:(t + 1) * C],
                    src[:, t * C:(t + 1) * C], 1.0, None, OP.mult, OP.add,
                    accum_out=st[:, mu0 + t:mu0 + t + 1])
        for t in range(NTT):
            xt = src[:, t * C:(t + 1) * C]
            self.stt(v, "v", dummy[:, t * C:(t + 1) * C], xt, 1.0, xt,
                     OP.mult, OP.mult,
                     accum_out=st[:, rs0 + t:rs0 + t + 1])
        for t in range(NTT):
            self.ts(v, "v", st[:, mu0 + t:mu0 + t + 1],
                    st[:, mu0 + t:mu0 + t + 1], 1.0 / C, None, OP.mult)
        for t in range(NTT):
            self.ts(v, "v", st[:, rs0 + t:rs0 + t + 1],
                    st[:, rs0 + t:rs0 + t + 1], 1.0 / C, None, OP.mult)
        for t in range(NTT):
            self.stt(v, "v", st[:, LT[t]:LT[t] + 1],
                     st[:, mu0 + t:mu0 + t + 1], -1.0,
                     st[:, mu0 + t:mu0 + t + 1], OP.mult, OP.mult)
        for t in range(NTT):
            self.tt(v, "v", st[:, rs0 + t:rs0 + t + 1],
                    st[:, rs0 + t:rs0 + t + 1], st[:, LT[t]:LT[t] + 1],
                    OP.add)

    def ln_rs_s(self, s, rs0):
        sl = self.strips[:, rs0:rs0 + self.NTT]
        self.act(s, "s", sl, sl, AF.Ln,
                 bias=self.strips[:, CEPS:CEPS + 1])
        self.act(s, "s", sl, sl, AF.Exp, scale=-0.5)

    def win_1a(self, w):
        b = self.block
        NCH, C, W, NTT, WPS = self.NCH, self.C, self.W, self.NTT, self.WPS
        H = self.H
        seq, wj = divmod(w, WPS)
        buf = w % 2
        st = self.strips
        tr0 = w * W // 128
        WL = W + 1
        pre_bf = self.scrbf
        h_f = self.scrbf
        rw_tm = self.pre_f

        @b.sync
        def _(sy):
            if w >= 2:
                self.wait_key(sy, ("x1sp", w - 2))
            i = sy.dma_start(
                out=self.x_tm[buf].ap().rearrange("p (t c) -> p t c", c=C),
                in_=self.x_d.ap().rearrange("(t p) c -> p t c", p=128)[
                    :, tr0:tr0 + NTT, :])
            self.mark(i, f"sy_x{buf}", ("xld", w), by=16)

        @b.vector
        def _(v):
            if w == 0:
                self.wait_key(v, ("setupv",))
                self.wait_key(v, ("wproj",))
                self.wait_key(v, ("vecs",))
            if w >= 1:
                self.wait_key(v, ("xnsp", w - 1))
            self.wait_key(v, ("xld", w))
            self.ln_stats_v(v, self.x_tm[buf], self.pre_f, MU0, RS0)
            self.ms(v, "v", "v_ms", ("lnov", w))

        @b.scalar
        def _(s):
            if w == 0:
                self.wait_key(s, ("setups",))
            self.wait_key(s, ("lnov", w))
            self.ln_rs_s(s, RS0)
            self.ms(s, "s", "sc_ms", ("lnos", w))

        @b.vector
        def _(v):
            self.wait_key(v, ("lnos", w))
            for t in range(NTT):
                self.ts(v, "v", self.x_tm[buf][:, t * C:(t + 1) * C],
                        self.x_tm[buf][:, t * C:(t + 1) * C],
                        st[:, MU0 + t:MU0 + t + 1], st[:, RS0 + t:RS0 + t + 1],
                        OP.subtract, OP.mult)
            for t in range(NTT):
                self.red(v, "v", self.amxa[:, t:t + 1],
                         self.x_tm[buf][:, t * C:(t + 1) * C], OP.max,
                         absval=True)
            self.red(v, "v", st[:, TMPA:TMPA + 1], self.amxa[:, 0:NTT],
                     OP.max)
            if w == 0:
                self.cp(v, "v", st[:, AM1:AM1 + 1], st[:, TMPA:TMPA + 1])
            else:
                self.tt(v, "v", st[:, AM1:AM1 + 1], st[:, AM1:AM1 + 1],
                        st[:, TMPA:TMPA + 1], OP.max)
            self.ms(v, "v", "v_ms", ("xn", w))
            for t in range(NTT):
                self.tt(v, "v", st[:, RS1 + t:RS1 + t + 1],
                        st[:, RS0 + t:RS0 + t + 1],
                        st[:, RS0 + t:RS0 + t + 1], OP.mult)
            for t in range(NTT):
                self.ts(v, "v", st[:, RS1 + t:RS1 + t + 1],
                        st[:, RS1 + t:RS1 + t + 1], -EPS_LN, 1.0 + EPS_LN,
                        OP.mult, OP.add)
            self.ms(v, "v", "v_ms", ("ln1v", w))

        @b.sync
        def _(sy):
            self.wait_key(sy, ("xn", w))
            i = sy.dma_start(
                out=self.xn_sp.ap().rearrange("(t p) c -> p t c", p=128)[
                    :, tr0:tr0 + NTT, :],
                in_=self.x_tm[buf].ap().rearrange("p (t c) -> p t c", c=C))
            self.mark(i, "sy_xnsp", ("xnsp", w), by=16)

        @b.scalar
        def _(s):
            self.wait_key(s, ("ln1v", w))
            sl = st[:, RS1:RS1 + NTT]
            self.act(s, "s", sl, sl, AF.Ln)
            self.act(s, "s", sl, sl, AF.Exp, scale=-0.5)
            self.ms(s, "s", "sc_ms", ("ln1s", w))

        @b.vector
        def _(v):
            self.wait_key(v, ("ln1s", w))
            for t in range(NTT):
                self.ts(v, "v", pre_bf[:, t * C:(t + 1) * C],
                        self.x_tm[buf][:, t * C:(t + 1) * C],
                        st[:, RS1 + t:RS1 + t + 1], None, OP.mult)
            self.ms(v, "v", "v_ms", ("pre1", w))

        @b.scalar
        def _(s):
            self.wait_key(s, ("pre1", w))
            for t in range(NTT):
                i = s.dma_start(
                    out=self.pre_f.ap().rearrange(
                        "p (c q) -> p c q", q=W)[:, :, t * 128:(t + 1) * 128],
                    in_=pre_bf.ap()[:, 0:NTT * C].rearrange(
                        "p (t c q) -> p (t c) q", q=128, c=NCH)[
                        :, t * NCH:(t + 1) * NCH, :],
                    transpose=True)
                self.mark(i, "sc_preT", by=16)
            self.setev("sc_preT", ("preT", w))

        @b.vector
        def _(v):
            vcs = self.vcs
            self.wait_key(v, ("preT", w))
            if w >= 1:
                self.wait_key(v, ("tsdone", w - 1))
            for c in range(NCH):
                self.ts(v, "v", h_f[:, c * WL + 1:(c + 1) * WL],
                        self.pre_f[:, c * W:(c + 1) * W],
                        vcs[:, 0 * NCH + c:0 * NCH + c + 1],
                        vcs[:, 1 * NCH + c:1 * NCH + c + 1],
                        OP.mult, OP.add)
            if wj == 0:
                for c in range(NCH):
                    self.mset(v, "v", h_f[:, c * WL:c * WL + 1], 0.0)
            else:
                for c in range(NCH):
                    self.cp(v, "v", h_f[:, c * WL:c * WL + 1],
                            self.hcar[:, c:c + 1])
            hv = h_f.ap()[:, 0:NCH * WL].rearrange("p (c q) -> p c q", q=WL)
            pf = self.pre_f.ap().rearrange("p (c q) -> p c q", q=W)
            self.tt(v, "v", pf, hv[:, :, 1:WL], hv[:, :, 0:W], OP.subtract)
            for dst, mrow in ((self.tsk, 4), (self.tsv, 5), (self.tsr, 6)):
                for c in range(NCH):
                    self.stt(v, "v", dst[:, c * W:(c + 1) * W],
                             self.pre_f[:, c * W:(c + 1) * W],
                             vcs[:, mrow * NCH + c:mrow * NCH + c + 1],
                             h_f[:, c * WL:c * WL + W],
                             OP.mult, OP.add)
            if wj != WPS - 1:
                for c in range(NCH):
                    self.cp(v, "v", self.hcar[:, c:c + 1],
                            h_f[:, c * WL + W:c * WL + W + 1])
            self.ms(v, "v", "v_ms", ("ts", w))

        @b.tensor
        def _(t):
            self.wait_key(t, ("ts", w))
            for c in range(NCH):
                for q, (wbuf, tsb) in enumerate(
                        ((self.wk_b, self.tsk), (self.wv_b, self.tsv),
                         (self.wr_b, self.tsr))):
                    pt, bk, u = self.take_bank(t, ev="s")
                    for cc in range(NCH):
                        i = t.matmul(
                            pt[:, 0:W],
                            wbuf[:, cc * C + c * 128:cc * C + (c + 1) * 128],
                            tsb[:, cc * W:(cc + 1) * W],
                            start=(cc == 0), stop=(cc == NCH - 1))
                    self.mark(i, "t_mm", ("mm", bk, u))
                    self.events[("pbank", w, c, q)] = (bk, u)
            self.ms(t, "t", "t_ms", ("tsdone", w))

        def evac_half(s, h):
            for c in range(h * H, (h + 1) * H):
                hc = c - h * H
                for q in range(3):
                    bk, u = self.events[("pbank", w, c, q)]
                    self.wait_key(s, ("mm", bk, u))
                    pt = self.psb[bk]
                    if q == 0:
                        i = self.act(s, "s", self.ek_w[:, c * W:(c + 1) * W],
                                     pt[:, 0:W], AF.Exp)
                    elif q == 1:
                        i = self.cp(s, "s", self.v_w[:, c * W:(c + 1) * W],
                                    pt[:, 0:W])
                    else:
                        i = self.act(s, "s",
                                     self.er_h[:, hc * W:(hc + 1) * W],
                                     pt[:, 0:W], AF.Exp, scale=-1.0)
                    self.bank_free_inc(i, bk, u)
            self.ms(s, "s", "sc_ms", ("evh", w, h))

        @b.scalar
        def _(s):
            evac_half(s, 0)

        @b.vector
        def _(v):
            self.wait_key(v, ("evh", w, 0))
            if wj == 0:
                self.mset(v, "v", self.carP[:, :], 0.0)
                self.mset(v, "v", self.carA[:, :], 0.0)
            self.wkv_part1_half(v, w, 0)
            self.ms(v, "v", "v_ms", ("p1", w, 0))

        @b.scalar
        def _(s):
            self.wait_key(s, ("p1", w, 0))
            evac_half(s, 1)
            self.lnexp_half(s, w, 0)

        @b.vector
        def _(v):
            self.wait_key(v, ("lex", w, 0))
            if w >= 1:
                self.wait_key(v, ("rwT", w - 1, 1))
            self.wkv_part2_half(v, w, 0)
            self.ms(v, "v", "v_ms", ("p2", w, 0))

        @b.sync
        def _(sy):
            self.wait_key(sy, ("p2", w, 0))
            for hc in range(H):
                i = sy.dma_start(
                    out=rw_tm.ap().rearrange("p (t cq) -> p t cq", cq=C)[
                        :, :, hc * 128:(hc + 1) * 128],
                    in_=self.rw_h[:, hc * W:(hc + 1) * W].rearrange(
                        "p (t q) -> p t q", q=128),
                    transpose=True)
                self.mark(i, "sy_rwT0", by=16)
            self.setev("sy_rwT0", ("rwT", w, 0))

        @b.vector
        def _(v):
            self.wait_key(v, ("evh", w, 1))
            self.wait_key(v, ("p2", w, 0))
            self.wkv_part1_half(v, w, 1)
            self.ms(v, "v", "v_ms", ("p1", w, 1))

        @b.scalar
        def _(s):
            self.wait_key(s, ("p1", w, 1))
            self.lnexp_half(s, w, 1)

        @b.vector
        def _(v):
            self.wait_key(v, ("lex", w, 1))
            self.wait_key(v, ("rwT", w, 0))
            self.wkv_part2_half(v, w, 1)
            self.ms(v, "v", "v_ms", ("p2", w, 1))

        @b.sync
        def _(sy):
            self.wait_key(sy, ("p2", w, 1))
            for hc in range(H):
                c = H + hc
                i = sy.dma_start(
                    out=rw_tm.ap().rearrange("p (t cq) -> p t cq", cq=C)[
                        :, :, c * 128:(c + 1) * 128],
                    in_=self.rw_h[:, hc * W:(hc + 1) * W].rearrange(
                        "p (t q) -> p t q", q=128),
                    transpose=True)
                self.mark(i, "sy_rwT1", by=16)
            self.setev("sy_rwT1", ("rwT", w, 1))

        @b.vector
        def _(v):
            self.wait_key(v, ("rwT", w, 1))
            self.wait_key(v, ("xnsp", w))
            for t in range(NTT):
                self.tt(v, "v", self.x_tm[buf][:, t * C:(t + 1) * C],
                        self.x_tm[buf][:, t * C:(t + 1) * C],
                        rw_tm[:, t * C:(t + 1) * C], OP.add)
            self.ms(v, "v", "v_ms", ("x1", w))

        @b.sync
        def _(sy):
            self.wait_key(sy, ("x1", w))
            i = sy.dma_start(
                out=self.x1_sp.ap().rearrange("(t p) c -> p t c", p=128)[
                    :, tr0:tr0 + NTT, :],
                in_=self.x_tm[buf].ap().rearrange("p (t c) -> p t c", c=C))
            self.mark(i, f"sy_x1sp{buf}", ("x1sp", w), by=16)

        if w == self.NWIN - 1:
            @b.vector
            def _(v):
                self.mark(v.nop(), "v_ms", ("amax", 1))

    def lnexp_half(self, s, w, h):
        H, W = self.H, self.W
        sl = slice(0, H * W)
        self.act(s, "s", self.LNW[:, sl], self.LNW[:, sl], AF.Ln)
        self.act(s, "s", self.LNW[:, sl], self.LNW[:, sl], AF.Exp, scale=-1.0)
        self.act(s, "s", self.LND[:, sl], self.LND[:, sl], AF.Ln)
        self.act(s, "s", self.LND[:, sl], self.LND[:, sl], AF.Exp, scale=-1.0)
        self.ms(s, "s", "sc_ms", ("lex", w, h))

    def wkv_part1_half(self, v, w, h):
        H, W, WL = self.H, self.W, self.W + 1
        cs = list(range(h * H, (h + 1) * H))

        def E(hc):
            c = cs[hc]
            ek = self.ek_w[:, c * W:(c + 1) * W]
            er = self.er_h[:, hc * W:(hc + 1) * W]
            wPt = self.wP_h[:, hc * WL:(hc + 1) * WL]
            wP = self.wP_h[:, hc * WL:(hc + 1) * WL]
            w0 = self.wb[0][:, hc * W:(hc + 1) * W]
            w1 = self.wb[1][:, hc * W:(hc + 1) * W]
            w2 = None
            Mw = self.LNW[:, hc * W:(hc + 1) * W]
            ddw = self.LND[:, hc * W:(hc + 1) * W]
            return c, ek, er, wPt, wP, w0, w1, w2, Mw, ddw

        def row(fn):
            for hc in range(H):
                fn(*E(hc))
        row(lambda c, ek, er, wPt, wP, w0, w1, w2, Mw, ddw:
            self.cp(v, "v", wP[:, 0:1], self.carP[:, c:c + 1]))
        row(lambda c, ek, er, wPt, wP, w0, w1, w2, Mw, ddw:
            self.scan(v, "v", wP[:, 1:WL], self.onesb[:, :], ek,
                      self.carP[:, c:c + 1], OP.mult, OP.add))
        row(lambda c, ek, er, wPt, wP, w0, w1, w2, Mw, ddw:
            self.cp(v, "v", self.carP[:, c:c + 1], wP[:, W:WL]))
        row(lambda c, ek, er, wPt, wP, w0, w1, w2, Mw, ddw:
            self.tt(v, "v", Mw, wP[:, 0:W], ek, OP.max))
        row(lambda c, ek, er, wPt, wP, w0, w1, w2, Mw, ddw:
            self.ts(v, "v", w0, ek, self.eu[:, c:c + 1], None, OP.mult))
        row(lambda c, ek, er, wPt, wP, w0, w1, w2, Mw, ddw:
            self.tt(v, "v", w1, wP[:, 0:W], w0, OP.add))
        row(lambda c, ek, er, wPt, wP, w0, w1, w2, Mw, ddw:
            self.stt(v, "v", ddw, er, 1.0, w1, OP.add, OP.mult))

    def wkv_part2_half(self, v, w, h):
        H, W, WL = self.H, self.W, self.W + 1
        cs = list(range(h * H, (h + 1) * H))

        def E(hc):
            c = cs[hc]
            ek = self.ek_w[:, c * W:(c + 1) * W]
            vv = self.v_w[:, c * W:(c + 1) * W]
            wP = self.wP_h[:, hc * WL:(hc + 1) * WL]
            wA = self.wA[:, hc * WL:(hc + 1) * WL]
            w0 = self.wb[0][:, hc * W:(hc + 1) * W]
            wf1 = self.wf[0][:, hc * W:(hc + 1) * W]
            wf2 = self.wf[1][:, hc * W:(hc + 1) * W]
            rM = self.LNW[:, hc * W:(hc + 1) * W]
            rdd = self.LND[:, hc * W:(hc + 1) * W]
            rw = self.rw_h[:, hc * W:(hc + 1) * W]
            return c, ek, vv, wP, wA, w0, wf1, wf2, rM, rdd, rw

        def row(fn):
            for hc in range(H):
                fn(*E(hc))
        row(lambda c, ek, vv, wP, wA, w0, wf1, wf2, rM, rdd, rw:
            self.tt(v, "v", wf1, wP[:, 0:W], rM, OP.mult))
        row(lambda c, ek, vv, wP, wA, w0, wf1, wf2, rM, rdd, rw:
            self.tt(v, "v", wf2, ek, rM, OP.mult))
        row(lambda c, ek, vv, wP, wA, w0, wf1, wf2, rM, rdd, rw:
            self.tt(v, "v", wf2, wf2, vv, OP.mult))
        row(lambda c, ek, vv, wP, wA, w0, wf1, wf2, rM, rdd, rw:
            self.cp(v, "v", wA[:, 0:1], self.carA[:, c:c + 1]))
        row(lambda c, ek, vv, wP, wA, w0, wf1, wf2, rM, rdd, rw:
            self.scan(v, "v", wA[:, 1:WL], wf1, wf2,
                      self.carA[:, c:c + 1], OP.mult, OP.add))
        row(lambda c, ek, vv, wP, wA, w0, wf1, wf2, rM, rdd, rw:
            self.cp(v, "v", self.carA[:, c:c + 1], wA[:, W:WL]))
        row(lambda c, ek, vv, wP, wA, w0, wf1, wf2, rM, rdd, rw:
            self.tt(v, "v", wf1, wP[:, 0:W], wA[:, 0:W], OP.mult))
        row(lambda c, ek, vv, wP, wA, w0, wf1, wf2, rM, rdd, rw:
            self.tt(v, "v", wf2, w0, vv, OP.mult))
        row(lambda c, ek, vv, wP, wA, w0, wf1, wf2, rM, rdd, rw:
            self.tt(v, "v", wf1, wf1, wf2, OP.add))
        row(lambda c, ek, vv, wP, wA, w0, wf1, wf2, rM, rdd, rw:
            self.tt(v, "v", rw, wf1, rdd, OP.mult))

    def emit_amax_collective(self, which):
        b = self.block
        st = self.strips
        amcol = AM1 if which == 1 else AM2
        rscol = RSX if which == 1 else RSH
        svcol = SXV if which == 1 else SHV
        clipcol = CLIPX if which == 1 else CLIPD
        cci = self.cc1i if which == 1 else self.cc2i
        cco = self.cc1o if which == 1 else self.cc2o

        @b.sync
        def _(sy):
            self.wait_key(sy, ("amax", which))
            i = sy.dma_start(
                out=self.am_sp[0:1, :].rearrange("o (p q) -> p (o q)", q=1),
                in_=st[:, amcol:amcol + 1])
            self.mark(i, "sy_am", by=16)
            self.wait_all(sy, "sy_am")
            i = sy.dma_start(out=self.amrow[:, :], in_=self.am_sp[0:1, :])
            self.mark(i, "sy_am", ("amrow", which), by=16)

        @b.vector
        def _(v):
            self.wait_key(v, ("amrow", which))
            self.red(v, "v", self.ccb[0:1, 0:1], self.amrow[:, :], OP.max)
            self.ms(v, "v", "v_ms", ("amloc", which))

        @b.gpsimd
        def _(g):
            self.wait_key(g, ("amloc", which))
            i = g.dma_start(out=cci[0:1, 0:8], in_=self.ccb[0:1, 0:8])
            self.mark(i, "g_cd", by=16)
            self.wait_all(g, "g_cd")
            i = g.collective_compute(
                "AllReduce", OP.max,
                replica_groups=[list(range(self.cfg["NCORE"]))],
                ins=[cci.ap().opt()], outs=[cco.ap().opt()])
            self.mark(i, "g_cc", by=1)
            self.wait_all(g, "g_cc")
            i = g.dma_start(out=self.ccb[0:1, 1:2], in_=cco[0:1, 0:1])
            self.mark(i, "g_cd", ("ccdone", which), by=16)

        @b.scalar
        def _(s):
            if which == 2:
                self.pin_act(s, set_id=6)
            self.wait_key(s, ("ccdone", which))
            self.wait_key(s, ("clips",))
            cl = st[0:1, clipcol:clipcol + 1]
            t0 = st[0:1, SIGT:SIGT + 1]
            self.act(s, "s", t0, cl, AF.Exp, scale=-1.0)
            self.act(s, "s", t0, t0, AF.Ln, bias=1.0)
            self.act(s, "s", t0, t0, AF.Exp, scale=-1.0)
            self.ms(s, "s", "sc_ms", ("sig", which))

        @b.vector
        def _(v):
            self.wait_key(v, ("sig", which))
            sv = st[0:1, svcol:svcol + 1]
            self.tt(v, "v", sv, st[0:1, SIGT:SIGT + 1], self.ccb[0:1, 1:2],
                    OP.mult)
            self.ts(v, "v", sv, sv, 1.0 / QMAX, None, OP.mult)
            self.recip(v, "v", st[0:1, rscol:rscol + 1], sv)
            self.ms(v, "v", "v_ms", ("scal0", which))

        @b.sync
        def _(sy):
            self.wait_key(sy, ("scal0", which))
            i = sy.dma_start(out=self.sc_sp[0:1, 0:1],
                             in_=st[0:1, svcol:svcol + 1])
            self.mark(i, "sy_am", by=16)
            i = sy.dma_start(out=self.sc_sp[0:1, 1:2],
                             in_=st[0:1, rscol:rscol + 1])
            self.mark(i, "sy_am", by=16)
            self.wait_all(sy, "sy_am")
            i = sy.dma_start(out=st[:, svcol:svcol + 1],
                             in_=self.sc_sp[0:1, 0:1].broadcast_to((128, 1)))
            self.mark(i, "sy_am", by=16)
            i = sy.dma_start(out=st[:, rscol:rscol + 1],
                             in_=self.sc_sp[0:1, 1:2].broadcast_to((128, 1)))
            self.mark(i, "sy_am", by=16)
            self.wait_all(sy, "sy_am")
            i = sy.nop()
            self.mark(i, "sy_sc", ("scales", which))

    def emit_1bkc(self, sc):
        b = self.block
        NCH, KCH, C, KC, W, NTT, WPS = (self.NCH, self.KCH, self.C, self.KC,
                                        self.W, self.NTT, self.WPS)

        self.wk_b2 = self.sb(sc, "wk_b2", [128, NCH * KC], BF16)
        self.wr_b2 = self.sb(sc, "wr_b2", [128, NCH * C], BF16)
        self.x_tm = [self.sb(sc, f"b_x{i}", [128, NTT * C], F32)
                     for i in range(2)]
        self.pre_bf = self.sb(sc, "b_pre", [128, NTT * C], BF16)
        self.pre_f = self.sb(sc, "b_pref", [128, NCH * W], BF16)
        self.h_f = self.sb(sc, "b_h", [128, NCH * (W + 1)], BF16)
        self.tsk2 = [self.sb(sc, f"b_tsk{i}", [128, NCH * W], BF16)
                     for i in range(2)]
        self.tsv2 = [self.sb(sc, f"b_tsv{i}", [128, NCH * W], BF16)
                     for i in range(2)]
        self.rcb = self.sb(sc, "b_rc", [128, NCH * W], BF16)
        self.rcs = self.sb(sc, "b_rcs", [128, NCH * W], BF16)
        self.kcb = [self.sb(sc, f"b_kc{i}", [128, W], BF16)
                    for i in range(16)]

        @b.gpsimd
        def _(g):
            for cc in range(NCH):
                i = g.dma_start(out=self.wk_b2[:, cc * KC:(cc + 1) * KC],
                                in_=self.wkcT_d[cc * 128:(cc + 1) * 128, :])
                self.mark(i, "g_ld", by=16)
                i = g.dma_start(out=self.wr_b2[:, cc * C:(cc + 1) * C],
                                in_=self.wrcT_d[cc * 128:(cc + 1) * 128, :])
                self.mark(i, "g_ld", by=16)
            self.setev("g_ld", ("wkc",))

        for w in range(self.NWIN):
            self.win_1bkc(w)

    def win_1bkc(self, w):
        b = self.block
        NCH, KCH, C, W, NTT, WPS = (self.NCH, self.KCH, self.C, self.W,
                                    self.NTT, self.WPS)
        seq, wj = divmod(w, WPS)
        buf = w % 2
        st = self.strips
        tr0 = w * W // 128

        @b.sync
        def _(sy):
            if w >= 2:
                self.wait_key(sy, ("pre2", w - 2))
            i = sy.dma_start(
                out=self.x_tm[buf].ap().rearrange("p (t c) -> p t c", c=C),
                in_=self.x1_sp.ap().rearrange("(t p) c -> p t c", p=128)[
                    :, tr0:tr0 + NTT, :])
            self.mark(i, f"sy_x{buf}", ("x1ld", w), by=16)

        @b.vector
        def _(v):
            self.wait_key(v, ("x1ld", w))
            if w == 0:
                self.wait_key(v, ("wkc",))
            if w >= 1:
                self.wait_key(v, ("preT2", w - 1))
            self.ln_stats_v(v, self.x_tm[buf], self.pre_bf, MU1, RS1)
            self.ms(v, "v", "v_ms", ("ln2v", w))

        @b.scalar
        def _(s):
            self.wait_key(s, ("ln2v", w))
            self.ln_rs_s(s, RS1)
            self.ms(s, "s", "sc_ms", ("ln2s", w))

        @b.vector
        def _(v):
            self.wait_key(v, ("ln2s", w))
            for t in range(NTT):
                self.ts(v, "v", self.pre_bf[:, t * C:(t + 1) * C],
                        self.x_tm[buf][:, t * C:(t + 1) * C],
                        st[:, MU1 + t:MU1 + t + 1], st[:, RS1 + t:RS1 + t + 1],
                        OP.subtract, OP.mult)
            self.ms(v, "v", "v_ms", ("pre2", w))

        @b.scalar
        def _(s):
            self.wait_key(s, ("pre2", w))
            for t in range(NTT):
                i = s.dma_start(
                    out=self.pre_f.ap().rearrange(
                        "p (c q) -> p c q", q=W)[:, :, t * 128:(t + 1) * 128],
                    in_=self.pre_bf.ap().rearrange(
                        "p (t c q) -> p (t c) q", q=128, c=NCH)[
                        :, t * NCH:(t + 1) * NCH, :],
                    transpose=True)
                self.mark(i, "sc_preT", by=16)
            self.setev("sc_preT", ("preT2", w))

        @b.vector
        def _(v):
            vcs = self.vcs
            self.wait_key(v, ("preT2", w))
            if w >= 2:
                self.wait_key(v, ("tsdone2", w - 2))
            tsk, tsv = self.tsk2[buf], self.tsv2[buf]
            for c in range(NCH):
                self.ts(v, "v", self.h_f[:, c * (W + 1) + 1:(c + 1) * (W + 1)],
                        self.pre_f[:, c * W:(c + 1) * W],
                        vcs[:, 2 * NCH + c:2 * NCH + c + 1],
                        vcs[:, 3 * NCH + c:3 * NCH + c + 1],
                        OP.mult, OP.add)
            if wj == 0:
                for c in range(NCH):
                    self.mset(v, "v",
                              self.h_f[:, c * (W + 1):c * (W + 1) + 1], 0.0)
            hv = self.h_f.ap().rearrange("p (c q) -> p c q", q=W + 1)
            pf = self.pre_f.ap().rearrange("p (c q) -> p c q", q=W)
            self.tt(v, "v", pf, hv[:, :, 1:W + 1], hv[:, :, 0:W], OP.subtract)
            for dst, mrow in ((tsk, 7), (tsv, 8)):
                for c in range(NCH):
                    self.stt(v, "v", dst[:, c * W:(c + 1) * W],
                             self.pre_f[:, c * W:(c + 1) * W],
                             vcs[:, mrow * NCH + c:mrow * NCH + c + 1],
                             self.h_f[:, c * (W + 1):c * (W + 1) + W],
                             OP.mult, OP.add)
            if wj != WPS - 1:
                for c in range(NCH):
                    self.cp(v, "v", self.h_f[:, c * (W + 1):c * (W + 1) + 1],
                            self.h_f[:, c * (W + 1) + W:c * (W + 1) + W + 1])
            self.ms(v, "v", "v_ms", ("ts2", w))

        @b.tensor
        def _(t):
            self.wait_key(t, ("ts2", w))
            for c in range(NCH):
                pt, bk, u = self.take_bank(t, ev="s")
                for cc in range(NCH):
                    i = t.matmul(pt[:, 0:W],
                                 self.wr_b2[:, cc * C + c * 128:
                                            cc * C + (c + 1) * 128],
                                 self.tsv2[buf][:, cc * W:(cc + 1) * W],
                                 start=(cc == 0), stop=(cc == NCH - 1))
                self.mark(i, "t_mm", ("mm", bk, u))
                self.events[("rcbank", w, c)] = (bk, u)
            for c in range(self.KCH):
                pt, bk, u = self.take_bank(t, ev="s")
                for cc in range(NCH):
                    i = t.matmul(pt[:, 0:W],
                                 self.wk_b2[:, cc * self.KC + c * 128:
                                            cc * self.KC + (c + 1) * 128],
                                 self.tsk2[buf][:, cc * W:(cc + 1) * W],
                                 start=(cc == 0), stop=(cc == NCH - 1))
                self.mark(i, "t_mm", ("mm", bk, u))
                self.events[("kcbank", w, c)] = (bk, u)
            self.ms(t, "t", "t_ms", ("tsdone2", w))

        @b.scalar
        def _(s):
            for c in range(NCH):
                bk, u = self.events[("rcbank", w, c)]
                self.wait_key(s, ("mm", bk, u))
                pt = self.psb[bk]
                i = self.act(s, "s", self.rcb[:, c * W:(c + 1) * W],
                             pt[:, 0:W], AF.Exp, scale=-1.0)
                self.bank_free_inc(i, bk, u)
            self.ms(s, "s", "sc_ms", ("rce", w))
            if w >= 1:
                self.wait_key(s, ("qsp1", w - 1))
            for c in range(16):
                bk, u = self.events[("kcbank", w, c)]
                self.wait_key(s, ("mm", bk, u))
                pt = self.psb[bk]
                i = self.act(s, "s", self.kcb[c][:, :], pt[:, 0:W], AF.Relu)
                self.bank_free_inc(i, bk, u)
            self.ms(s, "s", "sc_ms", ("kcr", w, 0))

        @b.vector
        def _(v):
            self.wait_key(v, ("rce", w))
            if w >= 1:
                self.wait_key(v, ("rcsp", w - 1))
            self.ts(v, "v", self.rcs[:, :], self.rcb[:, :], 1.0, None, OP.add)
            self.ms(v, "v", "v_ms", ("rc1", w))
            self.wait_key(v, ("kcr", w, 0))
            for c in range(16):
                self.tt(v, "v", self.kcb[c][:, :], self.kcb[c][:, :],
                        self.kcb[c][:, :], OP.mult)
            self.ms(v, "v", "v_ms", ("kcq", w, 0))

        @b.sync
        def _(sy):
            self.wait_key(sy, ("kcq", w, 0))
            for c in range(16):
                i = sy.dma_start(
                    out=self.q_sp[c * 128:(c + 1) * 128, w * W:(w + 1) * W],
                    in_=self.kcb[c][:, :])
                self.mark(i, "sy_qsp0", by=16)
            self.setev("sy_qsp0", ("qsp0", w))

        @b.scalar
        def _(s):
            self.wait_key(s, ("rc1", w))
            self.act(s, "s", self.rcb[:, :], self.rcs[:, :], AF.Ln)
            self.act(s, "s", self.rcs[:, :], self.rcb[:, :], AF.Exp,
                     scale=-1.0)
            self.ms(s, "s", "sc_ms", ("rc", w))
            self.wait_key(s, ("qsp0", w))
            for c in range(16, self.KCH):
                bk, u = self.events[("kcbank", w, c)]
                self.wait_key(s, ("mm", bk, u))
                pt = self.psb[bk]
                i = self.act(s, "s", self.kcb[c - 16][:, :], pt[:, 0:W],
                             AF.Relu)
                self.bank_free_inc(i, bk, u)
            self.ms(s, "s", "sc_ms", ("kcr", w, 1))

        @b.vector
        def _(v):
            self.wait_key(v, ("kcr", w, 1))
            for c in range(16):
                self.tt(v, "v", self.kcb[c][:, :], self.kcb[c][:, :],
                        self.kcb[c][:, :], OP.mult)
            self.ms(v, "v", "v_ms", ("kcq", w, 1))

        @b.sync
        def _(sy):
            self.wait_key(sy, ("rc", w))
            i = sy.dma_start(
                out=self.rc_sp.ap().rearrange("(c p) t -> p c t", p=128)[
                    :, :, w * W:(w + 1) * W],
                in_=self.rcs.ap().rearrange("p (c q) -> p c q", q=W))
            self.mark(i, "sy_rcsp", ("rcsp", w), by=16)
            self.wait_key(sy, ("kcq", w, 1))
            for c in range(16, self.KCH):
                i = sy.dma_start(
                    out=self.q_sp[c * 128:(c + 1) * 128, w * W:(w + 1) * W],
                    in_=self.kcb[c - 16][:, :])
                self.mark(i, "sy_qsp1", by=16)
            self.setev("sy_qsp1", ("qsp1", w))

    def emit_1bvv(self, sc):
        b = self.block
        NCH, KCH, C, W, NTT = self.NCH, self.KCH, self.C, self.W, self.NTT

        self.wv_b2 = self.sb(sc, "wv_b2", [128, KCH * C], BF16)
        self.qwin = self.sb(sc, "c_qwin", [128, KCH * W], BF16)
        self.x_tm = [self.sb(sc, f"c_xt{i}", [128, NTT * C], F32)
                     for i in range(2)]
        self.xn_tm = self.sb(sc, "c_x1b", [128, NTT * C], F32)
        self.rcb = self.sb(sc, "c_rc", [128, NCH * W], BF16)
        self.rwv = self.sb(sc, "c_rwv", [128, NCH * W], BF16)
        self.rw_tm = self.sb(sc, "c_rwtm", [128, NTT * C], BF16)

        @b.gpsimd
        def _(g):
            for cc in range(KCH):
                i = g.dma_start(out=self.wv_b2[:, cc * C:(cc + 1) * C],
                                in_=self.wvcT_d[cc * 128:(cc + 1) * 128, :])
                self.mark(i, "g_ld", by=16)
            self.setev("g_ld", ("wvc",))

        for w in range(self.NWIN):
            self.win_1bvv(w)

    def win_1bvv(self, w):
        b = self.block
        NCH, KCH, C, W, NTT = self.NCH, self.KCH, self.C, self.W, self.NTT
        buf = w % 2
        tr0 = w * W // 128

        @b.sync
        def _(sy):
            if w >= 1:
                self.wait_key(sy, ("vvdone", w - 1))
            i = sy.dma_start(
                out=self.qwin.ap().rearrange("p (c q) -> p c q", q=W),
                in_=self.q_sp.ap().rearrange("(c p) t -> p c t", p=128)[
                    :, :, w * W:(w + 1) * W])
            self.mark(i, "sy_qld", ("qld", w), by=16)
            if w >= 2:
                self.wait_key(sy, ("accsp", w - 2))
            i = sy.dma_start(
                out=self.x_tm[buf].ap().rearrange("p (t c) -> p t c", c=C),
                in_=self.x_d.ap().rearrange("(t p) c -> p t c", p=128)[
                    :, tr0:tr0 + NTT, :])
            self.mark(i, f"sy_x{buf}", ("xld2", w), by=16)
            if w >= 1:
                self.wait_key(sy, ("acc", w - 1))
            i = sy.dma_start(
                out=self.xn_tm.ap().rearrange("p (t c) -> p t c", c=C),
                in_=self.x1_sp.ap().rearrange("(t p) c -> p t c", p=128)[
                    :, tr0:tr0 + NTT, :])
            self.mark(i, "sy_x1l", ("x1ld2", w), by=16)
            if w >= 1:
                self.wait_key(sy, ("rcvall", w - 1))
            i = sy.dma_start(
                out=self.rcb.ap().rearrange("p (c q) -> p c q", q=W),
                in_=self.rc_sp.ap().rearrange("(c p) t -> p c t", p=128)[
                    :, :, w * W:(w + 1) * W])
            self.mark(i, "sy_rcl", ("rcld", w), by=16)

        @b.tensor
        def _(t):
            if w == 0:
                self.wait_key(t, ("wvc",))
            self.wait_key(t, ("qld", w))
            for c in range(NCH):
                pt, bk, u = self.take_bank(t, ev="v")
                for cc in range(KCH):
                    i = t.matmul(pt[:, 0:W],
                                 self.wv_b2[:, cc * C + c * 128:
                                            cc * C + (c + 1) * 128],
                                 self.qwin[:, cc * W:(cc + 1) * W],
                                 start=(cc == 0), stop=(cc == KCH - 1))
                self.mark(i, "t_mm", ("mm", bk, u))
                self.events[("vvbank", w, c)] = (bk, u)
            self.ms(t, "t", "t_ms", ("vvdone", w))

        @b.vector
        def _(v):
            self.wait_key(v, ("rcld", w))
            if w >= 1:
                self.wait_key(v, ("rcvTa", w - 1))
            for c in range(NCH):
                bk, u = self.events[("vvbank", w, c)]
                self.wait_key(v, ("mm", bk, u))
                pt = self.psb[bk]
                i = self.tt(v, "v", self.rwv[:, c * W:(c + 1) * W],
                            pt[:, 0:W], self.rcb[:, c * W:(c + 1) * W],
                            OP.mult)
                self.bank_free_inc(i, bk, u)
            self.ms(v, "v", "v_ms", ("rcvall", w))

        @b.scalar
        def _(s):
            self.wait_key(s, ("rcvall", w))
            if w >= 1:
                self.wait_key(s, ("acc", w - 1))
            for c in range(NCH):
                i = s.dma_start(
                    out=self.rw_tm.ap().rearrange(
                        "p (t cq) -> p t cq", cq=C)[
                        :, :, c * 128:(c + 1) * 128],
                    in_=self.rwv[:, c * W:(c + 1) * W].rearrange(
                        "p (t q) -> p t q", q=128),
                    transpose=True)
                self.mark(i, "sc_rwT0", by=16)
            self.setev("sc_rwT0", ("rcvTa", w))

        @b.vector
        def _(v):
            self.wait_key(v, ("rcvTa", w))
            self.wait_key(v, ("xld2", w))
            self.wait_key(v, ("x1ld2", w))
            for t in range(NTT):
                self.tt(v, "v", self.x_tm[buf][:, t * C:(t + 1) * C],
                        self.x_tm[buf][:, t * C:(t + 1) * C],
                        self.xn_tm[:, t * C:(t + 1) * C], OP.add)
            for t in range(NTT):
                self.tt(v, "v", self.x_tm[buf][:, t * C:(t + 1) * C],
                        self.x_tm[buf][:, t * C:(t + 1) * C],
                        self.rw_tm[:, t * C:(t + 1) * C], OP.add)
            self.ms(v, "v", "v_ms", ("acc", w))

        @b.sync
        def _(sy):
            self.wait_key(sy, ("acc", w))
            i = sy.dma_start(
                out=self.acc_sp.ap().rearrange("(t p) c -> p t c", p=128)[
                    :, tr0:tr0 + NTT, :],
                in_=self.x_tm[buf].ap().rearrange("p (t c) -> p t c", c=C))
            self.mark(i, f"sy_acsp{buf}", ("accsp", w), by=16)

    def emit_wq(self, pro):
        b = self.block
        C, IC = self.C, self.IC
        st = self.strips

        wqf = self.sb(pro, "wqf", [128, IC], F32)
        wqg = self.sb(pro, "wqg", [128, IC], F32)
        wqi = self.sb(pro, "wqi", [128, IC], I32)
        wqb = self.sb(pro, "wqb", [128, IC], BF16)
        wqt = self.sb(pro, "wqt", [128, IC], BF16)

        for name, wd, rows, kin, scs, spT in (
                ("g", self.wg_d, self.ICH, C, self.sc_g, self.wgT_sp),
                ("u", self.wu_d, self.ICH, C, self.sc_u, self.wuT_sp),
                ("d", self.wd_d, self.NCH, IC, self.sc_d, self.wdT_sp)):
            for r in range(rows):
                @b.sync
                def _(sy, r=r, wd=wd, kin=kin, name=name):
                    if not (name == "g" and r == 0):
                        self.wait_key(sy, ("wqv", self._prev_wq))
                    i = sy.dma_start(out=wqf[:, 0:kin],
                                     in_=wd[r * 128:(r + 1) * 128, :])
                    self.mark(i, "sy_wq", ("wqld", name, r), by=16)

                @b.vector
                def _(v, r=r, kin=kin, name=name, scs=scs):
                    self.wait_key(v, ("wqld", name, r))
                    wf = wqf[:, 0:kin]
                    self.red(v, "v", st[:, TMPA:TMPA + 1], wf, OP.max,
                             absval=True)
                    self.ts(v, "v", scs[:, r:r + 1], st[:, TMPA:TMPA + 1],
                            1.0 / QMAX, None, OP.mult)
                    self.recip(v, "v", st[:, TMPB:TMPB + 1], scs[:, r:r + 1])
                    self.ts(v, "v", wqg[:, 0:kin], wf, st[:, TMPB:TMPB + 1],
                            None, OP.mult)
                    self.cp(v, "v", wqi[:, 0:kin], wqg[:, 0:kin])
                    self.cp(v, "v", wqg[:, 0:kin], wqi[:, 0:kin])
                    self.ts(v, "v", wqb[:, 0:kin], wqg[:, 0:kin],
                            -(QMAX + 1.0), QMAX, OP.max, OP.min)
                    self.ms(v, "v", "v_ms", ("wqv", (name, r)))
                    self._prev_wq = (name, r)

                @b.scalar
                def _(s, r=r, kin=kin, name=name):
                    self.wait_key(s, ("wqv", (name, r)))
                    if not (name == "g" and r == 0):
                        self.wait_key(s, ("wqsp", self._prev_wqt))
                    i = s.dma_start(
                        out=wqt[:, 0:kin].rearrange("p (n q) -> p n q",
                                                         q=128),
                        in_=wqb[:, 0:kin].rearrange("p (n q) -> p n q",
                                                         q=128),
                        transpose=True)
                    self.mark(i, "sc_wqT", ("wqt", name, r), by=16)

                @b.sync
                def _(sy, r=r, kin=kin, name=name, spT=spT):
                    self.wait_key(sy, ("wqt", name, r))
                    i = sy.dma_start(
                        out=spT.ap().rearrange("(n p) q -> p n q", p=128)[
                            :, :, r * 128:(r + 1) * 128],
                        in_=wqt[:, 0:kin].rearrange("p (n q) -> p n q",
                                                         q=128))
                    self.mark(i, "sy_wqsp", ("wqsp", (name, r)), by=16)
                    self._prev_wqt = (name, r)
        self.setev("sy_wqsp", ("wqall",))

    def emit_pass2(self, sc):
        b = self.block
        NCH, ICH, C, IC, W, NTT = (self.NCH, self.ICH, self.C, self.IC,
                                   self.W, self.NTT)
        st = self.strips

        self.wg_b = self.sb(sc, "wg_b", [128, NCH * IC], BF16)
        self.wu_b = self.sb(sc, "wu_b", [128, NCH * IC], BF16)
        self.x_tm = [self.sb(sc, f"d_x{i}", [128, NTT * C], F32)
                     for i in range(2)]
        self.xqs = self.sb(sc, "d_s2", [128, NTT * C // 2], F32)
        self.xqi = self.sb(sc, "d_qi", [128, NTT * C // 2], I32)
        self.pre_bf = self.sb(sc, "d_pre", [128, NTT * C], BF16)
        self.pre_f = self.sb(sc, "d_pref", [128, NCH * W], BF16)
        self.sgb = self.sb(sc, "d_sg", [128, ICH * W], BF16)
        self.midb = [self.sb(sc, f"d_mid{i}", [128, W], F32)
                     for i in range(11)]
        self.amxb = self.sb(sc, "d_amx", [128, ICH], F32)
        if self.cfg.get("DBG"):
            self.dbgpb = self.sb(sc, "d_dbgpb", [128, W], F32)

        @b.sync
        def _(sy):
            self.wait_key(sy, ("wqall",))
            for cc in range(NCH):
                i = sy.dma_start(out=self.wg_b[:, cc * IC:(cc + 1) * IC],
                                 in_=self.wgT_sp[cc * 128:(cc + 1) * 128, :])
                self.mark(i, "sy_ld", by=16)
                i = sy.dma_start(out=self.wu_b[:, cc * IC:(cc + 1) * IC],
                                 in_=self.wuT_sp[cc * 128:(cc + 1) * 128, :])
                self.mark(i, "sy_ld", by=16)
            self.setev("sy_ld", ("wgu",))

        @b.vector
        def _(v):
            self.wait_key(v, ("scales", 1))
            self.ts(v, "v", self.cmb_g[:, :], self.sc_g[:, :],
                    st[:, SXV:SXV + 1], None, OP.mult)
            self.ts(v, "v", self.cmb_u[:, :], self.sc_u[:, :],
                    st[:, SXV:SXV + 1], None, OP.mult)
            self.ms(v, "v", "v_ms", ("cmb",))

        for w in range(self.NWIN):
            self.win_2(w)

        @b.vector
        def _(v):
            self.ms(v, "v", "v_ms", ("amax", 2))

    def win_2(self, w):
        b = self.block
        NCH, ICH, C, IC, W, NTT = (self.NCH, self.ICH, self.C, self.IC,
                                   self.W, self.NTT)
        buf = w % 2
        st = self.strips
        tr0 = w * W // 128

        @b.sync
        def _(sy):
            if w >= 2:
                self.wait_key(sy, ("xq", w - 2))
            i = sy.dma_start(
                out=self.x_tm[buf].ap().rearrange("p (t c) -> p t c", c=C),
                in_=self.xn_sp.ap().rearrange("(t p) c -> p t c", p=128)[
                    :, tr0:tr0 + NTT, :])
            self.mark(i, f"sy_x{buf}", ("xnld", w), by=16)

        @b.vector
        def _(v):
            self.wait_key(v, ("xnld", w))
            if w == 0:
                self.wait_key(v, ("wgu",))
                self.wait_key(v, ("cmb",))
            if w >= 1:
                self.wait_key(v, ("xqT", w - 1))
            if self.cfg.get("DBG") and w >= 1:
                self.wait_key(v, ("xqd", w - 1))
            xb = self.x_tm[buf]
            HF = NTT * C // 2
            for hh in range(2):
                sl = slice(hh * HF, (hh + 1) * HF)
                self.ts(v, "v", self.xqs[:, 0:HF], xb[:, sl],
                        st[:, RSX:RSX + 1], None, OP.mult)
                self.cp(v, "v", self.xqi[:, 0:HF], self.xqs[:, 0:HF])
                self.cp(v, "v", self.xqs[:, 0:HF], self.xqi[:, 0:HF])
                self.ts(v, "v", self.pre_bf[:, sl], self.xqs[:, 0:HF],
                        -(QMAX + 1.0), QMAX, OP.max, OP.min)
            self.ms(v, "v", "v_ms", ("xq", w))

        if self.cfg.get("DBG"):
            @b.sync
            def _(sy):
                self.wait_key(sy, ("xq", w))
                i = sy.dma_start(
                    out=self.xqd_sp.ap().rearrange("(t p) c -> p t c", p=128)[
                        :, tr0:tr0 + NTT, :],
                    in_=self.pre_bf.ap().rearrange("p (t c) -> p t c", c=C))
                self.mark(i, "sy_xqd", ("xqd", w), by=16)

        @b.scalar
        def _(s):
            self.wait_key(s, ("xq", w))
            if w >= 1:
                self.wait_key(s, ("gumm", w - 1))
            for t in range(NTT):
                i = s.dma_start(
                    out=self.pre_f.ap().rearrange(
                        "p (c q) -> p c q", q=W)[:, :, t * 128:(t + 1) * 128],
                    in_=self.pre_bf.ap().rearrange(
                        "p (t c q) -> p (t c) q", q=128, c=NCH)[
                        :, t * NCH:(t + 1) * NCH, :],
                    transpose=True)
                self.mark(i, "sc_preT", by=16)
            self.setev("sc_preT", ("xqT", w))

        if self.cfg.get("DBG") and w == 0:
            @b.sync
            def _(sy):
                self.wait_key(sy, ("xqT", 0))
                for cc in range(NCH):
                    i = sy.dma_start(
                        out=self.pf_sp[cc * 128:(cc + 1) * 128, :],
                        in_=self.pre_f[:, cc * W:(cc + 1) * W])
                    self.mark(i, "sy_pfd", by=16)
                self.setev("sy_pfd", ("pfd",))

        @b.tensor
        def _(t):
            self.wait_key(t, ("xqT", w))
            if self.cfg.get("DBG") and w == 1:
                self.wait_key(t, ("pfd",))
            for q, wb in ((0, self.wg_b), (1, self.wu_b)):
                for c in range(ICH):
                    pt, bk, u = self.take_bank(t, ev="s" if q == 0 else "v")
                    for cc in range(NCH):
                        i = t.matmul(pt[:, 0:W],
                                     wb[:, cc * IC + c * 128:
                                        cc * IC + (c + 1) * 128],
                                     self.pre_f[:, cc * W:(cc + 1) * W],
                                     start=(cc == 0), stop=(cc == NCH - 1))
                    self.mark(i, "t_mm", ("mm", bk, u))
                    self.events[("gub", w, c, q)] = (bk, u)
            self.ms(t, "t", "t_ms", ("gumm", w))

        if self.cfg.get("DBG") and w == 0:
            @b.vector
            def _(v):
                bk, u = self.events[("gub", 0, 0, 0)]
                self.wait_key(v, ("mm", bk, u))
                self.cp(v, "v", self.dbgpb[:, :], self.psb[bk][:, 0:W])
                self.ms(v, "v", "v_ms", ("pbcp",))

            @b.sync
            def _(sy):
                self.wait_key(sy, ("pbcp",))
                i = sy.dma_start(out=self.pb_sp[:, :], in_=self.dbgpb[:, :])
                self.mark(i, "sy_pbd", ("pbd",), by=16)
                i = sy.dma_start(out=self.pc_sp[:, 0:ICH],
                                 in_=self.cmb_g[:, :])
                self.mark(i, "sy_pbd", by=16)
                i = sy.dma_start(out=self.pc_sp[:, 32:32 + ICH],
                                 in_=self.cmb_u[:, :])
                self.mark(i, "sy_pbd", by=16)

        @b.scalar
        def _(s):
            self.pin_act(s, set_id=18)
            if self.cfg.get("DBG") and w >= 1:
                self.wait_key(s, ("sgd", w - 1))
            if w >= 1:
                self.wait_key(s, ("midall", w - 1))
            for c in range(ICH):
                bk, u = self.events[("gub", w, c, 0)]
                self.wait_key(s, ("mm", bk, u))
                if self.cfg.get("DBG") and w == 0 and c == 0:
                    self.wait_key(s, ("pbcp",))
                pt = self.psb[bk]
                i = self.act(s, "s", self.sgb[:, c * W:(c + 1) * W],
                             pt[:, 0:W], AF.Silu, scale=self.cmb_g[:, c:c + 1])
                self.bank_free_inc(i, bk, u)
            self.ms(s, "s", "sc_ms", ("sgall", w))

        if self.cfg.get("DBG"):
            @b.sync
            def _(sy):
                self.wait_key(sy, ("sgall", w))
                for c in range(ICH):
                    i = sy.dma_start(
                        out=self.sg_sp[c * 128:(c + 1) * 128,
                                       w * W:(w + 1) * W],
                        in_=self.sgb[:, c * W:(c + 1) * W])
                    self.mark(i, "sy_sgd", by=16)
                self.setev("sy_sgd", ("sgd", w))

        def mid_chunk(v, c):
            gbk, gu = self.events[("gub", w, c, 0)]
            self.wait_key(v, ("bkfree", gbk, gu))
            bk, u = self.events[("gub", w, c, 1)]
            self.wait_key(v, ("mm", bk, u))
            pt = self.psb[bk]
            i = self.stt(v, "v", self.midb[c % 11][:, :], pt[:, 0:W],
                         self.cmb_u[:, c:c + 1], self.sgb[:, c * W:(c + 1) * W],
                         OP.mult, OP.mult)
            self.bank_free_inc(i, bk, u)
            self.red(v, "v", self.amxb[:, c:c + 1], self.midb[c % 11][:, :],
                     OP.max, absval=True)

        @b.vector
        def _(v):
            if w >= 1:
                self.wait_key(v, ("misp1", w - 1))
            for c in range(0, 11):
                mid_chunk(v, c)
            self.ms(v, "v", "v_ms", ("mid0", w))

        @b.sync
        def _(sy):
            self.wait_key(sy, ("mid0", w))
            for c in range(0, 11):
                i = sy.dma_start(
                    out=self.mid_sp[c * 128:(c + 1) * 128,
                                    w * W:(w + 1) * W],
                    in_=self.midb[c % 11][:, :])
                self.mark(i, "sy_misp0", by=16)
            self.setev("sy_misp0", ("misp0", w))

        @b.vector
        def _(v):
            self.wait_key(v, ("misp0", w))
            for c in range(11, ICH):
                mid_chunk(v, c)
            self.red(v, "v", st[:, TMPA:TMPA + 1], self.amxb[:, 0:ICH],
                     OP.max)
            if w == 0:
                self.cp(v, "v", st[:, AM2:AM2 + 1], st[:, TMPA:TMPA + 1])
            else:
                self.tt(v, "v", st[:, AM2:AM2 + 1], st[:, AM2:AM2 + 1],
                        st[:, TMPA:TMPA + 1], OP.max)
            self.ms(v, "v", "v_ms", ("mid1", w))
            self.ms(v, "v", "v_ms", ("midall", w))

        @b.sync
        def _(sy):
            self.wait_key(sy, ("mid1", w))
            for c in range(11, ICH):
                i = sy.dma_start(
                    out=self.mid_sp[c * 128:(c + 1) * 128,
                                    w * W:(w + 1) * W],
                    in_=self.midb[c % 11][:, :])
                self.mark(i, "sy_misp1", by=16)
            self.setev("sy_misp1", ("misp1", w))

    def emit_pass3(self, sc):
        b = self.block
        NCH, ICH, C, W, NTT = self.NCH, self.ICH, self.C, self.W, self.NTT
        st = self.strips

        self.wd_b = self.sb(sc, "wd_b", [128, ICH * C], BF16)
        self.x_tm = [self.sb(sc, f"e_x{i}", [128, NTT * C], F32)
                     for i in range(2)]
        self.mldbB = self.sb(sc, "e_ml", [128, 8 * W], F32)
        self.qsf = self.sb(sc, "e_qf", [128, 4 * W], F32)
        self.qsi = self.sb(sc, "e_qi", [128, 4 * W], I32)
        self.hqw = [self.sb(sc, f"e_hq{i}", [128, ICH * W], BF16)
                    for i in range(2)]
        self.rw_bf = [self.sb(sc, f"e_rw{i}", [128, W], BF16)
                      for i in range(2)]
        self.rw_tm = self.sb(sc, "e_rwtm", [128, NTT * C], BF16)

        @b.sync
        def _(sy):
            for cc in range(ICH):
                i = sy.dma_start(out=self.wd_b[:, cc * C:(cc + 1) * C],
                                 in_=self.wdT_sp[cc * 128:(cc + 1) * 128, :])
                self.mark(i, "sy_ld", by=16)
            self.setev("sy_ld", ("wdT",))

        @b.vector
        def _(v):
            self.wait_key(v, ("scales", 2))
            self.ts(v, "v", self.cmb_d[:, :], self.sc_d[:, :],
                    st[:, SHV:SHV + 1], None, OP.mult)
            self.ms(v, "v", "v_ms", ("cmbd",))

        for w in range(self.NWIN):
            self.win_3(w)

    def win_3(self, w):
        b = self.block
        NCH, ICH, C, W, NTT = self.NCH, self.ICH, self.C, self.W, self.NTT
        buf = w % 2
        st = self.strips
        tr0 = w * W // 128

        GW = 4
        groups = [(g0, min(g0 + GW, ICH)) for g0 in range(0, ICH, GW)]
        NG = len(groups)
        for gi, (c0, c1) in enumerate(groups):
            @b.sync
            def _(sy, gi=gi, c0=c0, c1=c1):
                if gi >= 2:
                    self.wait_key(sy, ("hqg", w, gi - 2))
                elif w >= 1:
                    self.wait_key(sy, ("hqg", w - 1, NG - 2 + gi))
                pb = (gi % 2) * GW
                for c in range(c0, c1):
                    par = c % 8
                    i = sy.dma_start(
                        out=self.mldbB[:, (pb + c - c0) * W:
                                       (pb + c - c0 + 1) * W],
                        in_=self.mid_sp[c * 128:(c + 1) * 128,
                                        w * W:(w + 1) * W])
                    self.mark(i, f"sy_ml{par}", ("mld", w, c), by=16)

            @b.vector
            def _(v, gi=gi, c0=c0, c1=c1):
                if gi == 0:
                    if w == 0:
                        self.wait_key(v, ("cmbd",))
                    if w >= 2:
                        self.wait_key(v, ("dmm", w - 2))
                hq = self.hqw[w % 2]
                for c in range(c0, c1):
                    self.wait_key(v, ("mld", w, c))
                pb = (gi % 2) * GW
                gw = c1 - c0
                sl = slice(pb * W, (pb + gw) * W)
                self.ts(v, "v", self.qsf[:, 0:gw * W], self.mldbB[:, sl],
                        st[:, RSH:RSH + 1], None, OP.mult)
                self.cp(v, "v", self.qsi[:, 0:gw * W], self.qsf[:, 0:gw * W])
                self.cp(v, "v", self.qsf[:, 0:gw * W], self.qsi[:, 0:gw * W])
                self.ts(v, "v", hq[:, c0 * W:c1 * W], self.qsf[:, 0:gw * W],
                        -(QMAX + 1.0), QMAX, OP.max, OP.min)
                self.ms(v, "v", "v_ms", ("hqg", w, gi))

        for co in range(NCH):
            @b.tensor
            def _(t, co=co):
                if co == 0 and w == 0:
                    self.wait_key(t, ("wdT",))
                hq = self.hqw[w % 2]
                pt, bk, u = self.take_bank(t, ev="v")
                for c in range(ICH):
                    if co == 0 and c % 4 == 0:
                        self.wait_key(t, ("hqg", w, c // 4))
                    i = t.matmul(pt[:, 0:W],
                                 self.wd_b[:, c * C + co * 128:
                                           c * C + (co + 1) * 128],
                                 hq[:, c * W:(c + 1) * W],
                                 start=(c == 0), stop=(c == ICH - 1))
                self.mark(i, "t_mm", ("mm", bk, u))
                self.events[("dwb", w, co)] = (bk, u)
                if co == NCH - 1:
                    self.ms(t, "t", "t_ms", ("dmm", w))

            @b.vector
            def _(v, co=co):
                if co >= 2:
                    self.wait_key(v, ("dwT", w, co - 2))
                bk, u = self.events[("dwb", w, co)]
                self.wait_key(v, ("mm", bk, u))
                pt = self.psb[bk]
                i = self.ts(v, "v", self.rw_bf[co % 2][:, :], pt[:, 0:W],
                            self.cmb_d[:, co:co + 1], None, OP.mult)
                self.bank_free_inc(i, bk, u)
                self.ms(v, "v", "v_ms", ("dwn", w, co))

            @b.scalar
            def _(s, co=co):
                self.wait_key(s, ("dwn", w, co))
                if co == 0 and w >= 1:
                    self.wait_key(s, ("outv", w - 1))
                i = s.dma_start(
                    out=self.rw_tm.ap().rearrange(
                        "p (t cq) -> p t cq", cq=C)[
                        :, :, co * 128:(co + 1) * 128],
                    in_=self.rw_bf[co % 2].ap().rearrange("p (t q) -> p t q",
                                                          q=128),
                    transpose=True)
                self.mark(i, f"sc_rwT{co % 2}", ("dwT", w, co), by=16)

        @b.sync
        def _(sy):
            if w >= 2:
                self.wait_key(sy, ("outw", w - 2))
            i = sy.dma_start(
                out=self.x_tm[buf].ap().rearrange("p (t c) -> p t c", c=C),
                in_=self.acc_sp.ap().rearrange("(t p) c -> p t c", p=128)[
                    :, tr0:tr0 + NTT, :])
            self.mark(i, f"sy_x{buf}", ("accld", w), by=16)

        @b.vector
        def _(v):
            self.wait_key(v, ("accld", w))
            self.wait_key(v, ("dwT", w, NCH - 1))
            self.wait_key(v, ("dwT", w, NCH - 2))
            for t in range(NTT):
                self.tt(v, "v", self.x_tm[buf][:, t * C:(t + 1) * C],
                        self.x_tm[buf][:, t * C:(t + 1) * C],
                        self.rw_tm[:, t * C:(t + 1) * C], OP.add)
            self.ms(v, "v", "v_ms", ("outv", w))

        @b.sync
        def _(sy):
            self.wait_key(sy, ("outv", w))
            i = sy.dma_start(
                out=self.out_d.ap().rearrange("(t p) c -> p t c", p=128)[
                    :, tr0:tr0 + NTT, :],
                in_=self.x_tm[buf].ap().rearrange("p (t c) -> p t c", c=C))
            self.mark(i, f"sy_out{w % 2}", ("outw", w), by=16)
            if w == self.NWIN - 1:
                if self.cfg.get("DBG"):
                    i = sy.dma_start(out=self.dbg_sp[:, :],
                                     in_=self.strips[:, :])
                    self.mark(i, "sy_out0", by=16)
                self.wait_all(sy, "sy_out0")
                self.wait_all(sy, "sy_out1")


def build_nc(cfg=None):
    cfg = dict(DEF_CFG, **(cfg or {}))
    return Builder(cfg).build()


def prep_inputs(inputs, cfg=None):
    cfg = dict(DEF_CFG, **(cfg or {}))
    NSEQ, T, C, NCORE = cfg["NSEQ"], cfg["T"], cfg["C"], cfg["NCORE"]
    x = np.ascontiguousarray(np.asarray(inputs["L0_attn_x"], np.float32))
    B = x.shape[0]
    _vl = [
        np.asarray(inputs["L0_attn_ln1_g"], np.float32),
        np.asarray(inputs["L0_attn_ln1_b"], np.float32),
        np.asarray(inputs["L0_attn_ln2_g"], np.float32),
        np.asarray(inputs["L0_attn_ln2_b"], np.float32),
        np.asarray(inputs["L0_attn_mix_k"], np.float32),
        np.asarray(inputs["L0_attn_mix_v"], np.float32),
        np.asarray(inputs["L0_attn_mix_r"], np.float32),
        np.asarray(inputs["L0_attn_cmix_k"], np.float32),
        np.asarray(inputs["L0_attn_cmix_r"], np.float32),
        np.asarray(inputs["L0_attn_time_first"], np.float32),
    ]
    NCH = C // 128
    vecs = np.ascontiguousarray(
        np.stack(_vl, axis=0).reshape(10, NCH, 128)
        .transpose(2, 0, 1).reshape(128, 10 * NCH))
    clips = np.ascontiguousarray(np.array([[
        float(np.asarray(inputs["L0_ffn_clip_x"]).reshape(-1)[0]),
        float(np.asarray(inputs["L0_ffn_clip_down"]).reshape(-1)[0])]],
        np.float32))
    shared = {
        "vecs": vecs, "clips": clips,
        "wkT": np.ascontiguousarray(
            np.asarray(inputs["L0_attn_Wk_t"], np.float32).T),
        "wvT": np.ascontiguousarray(
            np.asarray(inputs["L0_attn_Wv_t"], np.float32).T),
        "wrT": np.ascontiguousarray(
            np.asarray(inputs["L0_attn_Wr_t"], np.float32).T),
        "wkcT": np.ascontiguousarray(
            np.asarray(inputs["L0_attn_Wk_c"], np.float32).T),
        "wvcT": np.ascontiguousarray(
            np.asarray(inputs["L0_attn_Wv_c"], np.float32).T),
        "wrcT": np.ascontiguousarray(
            np.asarray(inputs["L0_attn_Wr_c"], np.float32).T),
        "wg": np.ascontiguousarray(
            np.asarray(inputs["L0_ffn_w_gate"], np.float32)),
        "wu": np.ascontiguousarray(
            np.asarray(inputs["L0_ffn_w_up"], np.float32)),
        "wd": np.ascontiguousarray(
            np.asarray(inputs["L0_ffn_w_down"], np.float32)),
    }
    per = B // NCORE
    in_maps = []
    for i in range(NCORE):
        m = dict(shared)
        m["x"] = np.ascontiguousarray(
            x[i * per:(i + 1) * per].reshape(per * T, C))
        in_maps.append(m)
    return in_maps, per


def kernel(**inputs):
    cfg = dict(DEF_CFG)
    in_maps, per = prep_inputs(inputs, cfg)
    nc = build_nc(cfg)
    res = run_bass_kernel_spmd(nc, in_maps,
                               core_ids=list(range(cfg["NCORE"])))
    T, C = cfg["T"], cfg["C"]
    out = np.concatenate(
        [np.asarray(r["out"]).reshape(per, T, C) for r in res.results],
        axis=0)
    return out
